# version 16
# baseline (speedup 1.0000x reference)
import numpy as np
import concourse.bass as bass
import concourse.mybir as mybir
from concourse.bass import IndirectOffsetOnAxis
from concourse.ap import AP

F32 = mybir.dt.float32
I32 = mybir.dt.int32
U8 = mybir.dt.uint8
AF = mybir.ActivationFunctionType
ALU = mybir.AluOpType
AX = mybir.AxisListType

H, HD, D, CHUNK, NCH = 16, 64, 1024, 128, 16
T = CHUNK * NCH
E, FF, CAP = 8, 4096, 3072
NCORE = 8
REG = 2048
RW = 1032

TPC, APC, DPC, SSPC = 11, 7, 12, 48


class AutoDrain:
    _OPS = {"tensor_tensor", "tensor_scalar", "tensor_scalar_add",
            "tensor_scalar_mul", "tensor_copy", "tensor_reduce", "reduce_sum",
            "reduce_max", "reciprocal", "activation", "scalar_tensor_tensor",
            "memset", "iota", "select", "copy_predicated", "max", "max_index",
            "tensor_single_scalar", "tensor_add", "tensor_sub", "tensor_mul"}
    _OUT_KW = ("out", "ap")

    def __init__(self, eng):
        self._e = eng
        self._written = []
        self._read = []

    @staticmethod
    def _foot(ap):
        try:
            name = ap.tensor.name
            pdim = ap.ap[0]
            p0 = ap.base_partition() if callable(getattr(ap, "base_partition", None)) else 0
            np_ = pdim[1]
            steps = [(s, c) for s, c in ap.ap[1:]]
            span = 0
            for s, c in steps:
                span += abs(s) * (c - 1)
            item = 4
            off = ap.offset % ap.ap[0][0] if ap.ap[0][0] else ap.offset
            return (name, p0, p0 + np_, off * item, (off + span + 1) * item)
        except Exception:
            return (getattr(getattr(ap, "tensor", None), "name", "?"),
                    0, 128, 0, 1 << 40)

    @classmethod
    def _olap(cls, a, b):
        return (a[0] == b[0] and a[1] < b[2] and b[1] < a[2]
                and a[3] < b[4] and b[3] < a[4])

    def _aps_of(self, args, kwargs):
        from concourse.ap import AP as _AP
        ins, outs = [], []
        for k, v in kwargs.items():
            if isinstance(v, _AP):
                if k in ("out", "accum_out", "out_ap", "ap"):
                    outs.append(v)
                else:
                    ins.append(v)
        first = True
        for a in args:
            if isinstance(a, _AP):
                if first and not any(k in kwargs for k in ("out", "out_ap", "ap")):
                    outs.append(a)
                else:
                    ins.append(a)
                first = False
        return ins, outs

    def __getattr__(self, name):
        attr = getattr(self._e, name)
        if name == "drain":
            def wrapped_drain(*a, **k):
                self._written = []
                self._read = []
                return attr(*a, **k)
            return wrapped_drain
        if name == "wait_ge":
            return attr
        if name in self._OPS:
            def wrapped(*a, **k):
                ins, outs = self._aps_of(a, k)
                fin = [self._foot(x) for x in ins]
                fout = [self._foot(x) for x in outs]
                need = (any(self._olap(w, f) for w in self._written
                            for f in fin + fout)
                        or any(self._olap(r, f) for r in self._read for f in fout))
                if need:
                    self._e.drain()
                    self._written = []
                    self._read = []
                ins2 = attr(*a, **k)
                self._written.extend(fout)
                self._read.extend(fin)
                return ins2
            return wrapped
        return attr


def build_nc(debug=0):
    nc = bass.Bass("TRN2", target_bir_lowering=False, debug=False)

    xT = nc.dram_tensor("xT", [NCH, D, CHUNK], F32, kind="ExternalInput").ap()
    x0r = nc.dram_tensor("x0r", [T, D], F32, kind="ExternalInput").ap()
    wq = nc.dram_tensor("wq", [D, D], F32, kind="ExternalInput").ap()
    wk = nc.dram_tensor("wk", [D, D], F32, kind="ExternalInput").ap()
    wv = nc.dram_tensor("wv", [D, D], F32, kind="ExternalInput").ap()
    wsm = nc.dram_tensor("wsm", [D, 88], F32, kind="ExternalInput").ap()
    wsmb = nc.dram_tensor("wsmb", [1, 88], F32, kind="ExternalInput").ap()
    wo = nc.dram_tensor("wo", [D, D], F32, kind="ExternalInput").ap()
    woT = nc.dram_tensor("woT", [D, D], F32, kind="ExternalInput").ap()
    rT = nc.dram_tensor("rT", [D, E], F32, kind="ExternalInput").ap()
    w1 = nc.dram_tensor("w1", [E, 8, 128, 4096], F32, kind="ExternalInput").ap()
    w2 = nc.dram_tensor("w2", [E, 8, 128, 4096], F32, kind="ExternalInput").ap()
    b1T = nc.dram_tensor("b1T", [E, 128, 32], F32, kind="ExternalInput").ap()
    b2T = nc.dram_tensor("b2T", [E, 128, 8], F32, kind="ExternalInput").ap()
    segp = nc.dram_tensor("segp", [32, 8], F32, kind="ExternalInput").ap()
    cntp = nc.dram_tensor("cntp", [8, 1], F32, kind="ExternalInput").ap()

    qfsp = nc.dram_tensor("qfsp", [NCH, D, CHUNK], F32,
                          kind="ExternalOutput" if debug else "Internal").ap()
    qssp = nc.dram_tensor("qssp", [NCH, D, CHUNK], F32,
                          kind="ExternalOutput" if debug else "Internal").ap()
    otsp = nc.dram_tensor("otsp", [NCH, D, CHUNK], F32,
                          kind="ExternalOutput" if debug else "Internal").ap()
    bufd = nc.dram_tensor("bufd", [E * REG, RW], F32).ap()
    cc1i = nc.dram_tensor("cc1i", [32, 4097], F32).ap()
    cc1o = nc.dram_tensor("cc1o", [NCORE * 32, 4097], F32).ap()
    cc2i = nc.dram_tensor("cc2i", [1, 8], F32).ap()
    cc2o = nc.dram_tensor("cc2o", [NCORE, 8], F32).ap()
    sentb = nc.dram_tensor("sentb", [32, 4096], F32).ap()
    junkd = nc.dram_tensor("junkd", [64, 10, 16], F32).ap()

    y = nc.dram_tensor("y", [T, D], F32, kind="ExternalOutput").ap()
    yb = nc.dram_tensor("yb", [T + 128, D], F32).ap()
    if debug:
        dbg_state = nc.dram_tensor("dbg_state", [128, 1024], F32, kind="ExternalOutput").ap()
        dbg_gall = nc.dram_tensor("dbg_gall", [32, NCH], F32, kind="ExternalOutput").ap()
        dbg_gloc = nc.dram_tensor("dbg_gloc", [32, NCH + 1], F32, kind="ExternalOutput").ap()
        dbg_lx0 = nc.dram_tensor("dbg_lx0", [128, NCH * 8], F32, kind="ExternalOutput").ap()
        dbg_cc1o = nc.dram_tensor("dbg_cc1o", [NCORE * 32, 4097], F32, kind="ExternalOutput").ap()
        dbg_x1 = nc.dram_tensor("dbg_x1", [128, NCH * RW], F32, kind="ExternalOutput").ap()
        dbg_oh = nc.dram_tensor("dbg_oh", [128, NCH * 8], F32, kind="ExternalOutput").ap()
        dbg_dest = nc.dram_tensor("dbg_dest", [128, NCH], I32, kind="ExternalOutput").ap()
        dbg_rows = nc.dram_tensor("dbg_rows", [8, 64], F32, kind="ExternalOutput").ap()

    sb_const = nc.alloc_sbuf_tensor("c_ident", [128, 128], F32).ap()
    sb_u128 = nc.alloc_sbuf_tensor("c_u128", [128, 128], F32).ap()
    sb_ones_c = nc.alloc_sbuf_tensor("c_ones_c", [128, 1], F32).ap()
    sb_ones_r = nc.alloc_sbuf_tensor("c_ones_r", [1, 128], F32).ap()
    sb_iota_c = nc.alloc_sbuf_tensor("c_iota_c", [128, 1], F32).ap()
    sb_iota8 = nc.alloc_sbuf_tensor("c_iota8", [128, 8], F32).ap()
    sb_rowi = nc.alloc_sbuf_tensor("c_rowi", [128, 128], F32).ap()
    sb_coli = nc.alloc_sbuf_tensor("c_coli", [128, 128], F32).ap()
    sb_esel = nc.alloc_sbuf_tensor("c_esel", [16, 8 * 128], F32).ap()
    sb_m5 = nc.alloc_sbuf_tensor("c_m5", [128, 1], F32).ap()
    sb_eps = nc.alloc_sbuf_tensor("c_eps", [128, 1], F32).ap()

    sb_gloc = nc.alloc_sbuf_tensor("p1_gloc", [32, NCH + 1], F32).ap()
    sb_gall = nc.alloc_sbuf_tensor("p1_gall", [32, NCH], F32).ap()
    sb_lx0 = nc.alloc_sbuf_tensor("p1_lx0", [128, NCH * 8], F32).ap()
    sb_zero = nc.alloc_sbuf_tensor("c_zero", [128, RW], F32).ap()
    sb_o128 = nc.alloc_sbuf_tensor("c_o128", [128, 128], F32).ap()
    sb_adder = nc.alloc_sbuf_tensor("c_adder", [128, 8], F32).ap()
    sb_thrbc = nc.alloc_sbuf_tensor("p4_thrbc", [128, 8], F32).ap()
    sb_cntbc = nc.alloc_sbuf_tensor("p4_cntbc", [128, 8], F32).ap()
    sb_cct = nc.alloc_sbuf_tensor("p4_cct", [1, 8], I32).ap()
    sb_ctt = nc.alloc_sbuf_tensor("p4_ctt", [1, 8], I32).ap()
    sb_cti = nc.alloc_sbuf_tensor("p4_cti", [1, 8], I32).ap()
    sb_nff = nc.alloc_sbuf_tensor("p4_nff", [1, 8], F32).ap()
    sb_sbrow = nc.alloc_sbuf_tensor("p4_sbrow", [1, 64], F32).ap()
    sb_sbase = nc.alloc_sbuf_tensor("p4_sbase", [128, 64], F32).ap()
    sb_tidx = nc.alloc_sbuf_tensor("p4_tidx", [128, 32], I32).ap()
    sb_e2k = nc.alloc_sbuf_tensor("p4_e2k", [1, 32], F32).ap()
    sb_io4r = nc.alloc_sbuf_tensor("p4_io4r", [1, 32], F32).ap()
    sb_cntrow = nc.alloc_sbuf_tensor("p4_cntrow", [1, 8], F32).ap()

    sx0 = nc.alloc_semaphore("sx0")
    sx1 = nc.alloc_semaphore("sx1")
    sspq = nc.alloc_semaphore("sspq")
    sspo = nc.alloc_semaphore("sspo")
    scw = nc.alloc_semaphore("scw")
    szf = nc.alloc_semaphore("szf")
    sed = nc.alloc_semaphore("sed")
    set_ = nc.alloc_semaphore("set")
    sea = nc.alloc_semaphore("sea")
    sg2 = nc.alloc_semaphore("sg2")
    sco = nc.alloc_semaphore("sco")
    syc0 = nc.alloc_semaphore("syc0")
    syc1 = nc.alloc_semaphore("syc1")
    ssc = nc.alloc_semaphore("ssc")
    sp4 = nc.alloc_semaphore("sp4")
    ssj = nc.alloc_semaphore("ssj")
    ssb = nc.alloc_semaphore("ssb")
    s2i = nc.alloc_semaphore("s2i")
    s2o = nc.alloc_semaphore("s2o")
    tp = nc.alloc_semaphore("tp")
    ap_ = nc.alloc_semaphore("apx")
    dp = nc.alloc_semaphore("dpx")
    gp = nc.alloc_semaphore("gpx")
    ssp = nc.alloc_semaphore("ssp")
    ccs = nc.alloc_semaphore("ccs")
    sq = nc.alloc_semaphore("sqx")

    with nc.Block() as blk:
        @blk.gpsimd
        def _(g):
            g = AutoDrain(g)
            g.iota(sb_iota_c[:, :], pattern=[[0, 1]], base=0, channel_multiplier=1,
                   allow_small_or_imprecise_dtypes=True)
            g.iota(sb_iota8[:, :], pattern=[[1, 8]], base=0, channel_multiplier=0,
                   allow_small_or_imprecise_dtypes=True)
            g.iota(sb_rowi[:, :], pattern=[[0, 128]], base=0, channel_multiplier=1,
                   allow_small_or_imprecise_dtypes=True)
            g.iota(sb_coli[:, :], pattern=[[1, 128]], base=0, channel_multiplier=0,
                   allow_small_or_imprecise_dtypes=True)
            g.memset(sb_ones_c[:, :], 1.0)
            g.memset(sb_ones_r[:, :], 1.0)
            g.memset(sb_m5[:, :], -5.0)
            g.memset(sb_eps[:, :], 1e-5)
            g.memset(sb_zero[:, :], 0.0)
            g.memset(sb_o128[:, :], 1.0)
            g.memset(sb_gloc[:, :], 1.0)
            g.iota(sb_esel[:, :], pattern=[[2, 8], [1, 2], [0, 64]], base=0,
                   channel_multiplier=0, allow_small_or_imprecise_dtypes=True)
            g.iota(sb_io4r[:, :], pattern=[[0, 8], [128, 4]], base=0,
                   channel_multiplier=0, allow_small_or_imprecise_dtypes=True)
            g.iota(sb_e2k[:, :], pattern=[[2048, 8], [0, 4]], base=0,
                   channel_multiplier=0, allow_small_or_imprecise_dtypes=True)
            g.drain().then_inc(gp, 1)

        @blk.vector
        def _(v):
            v = AutoDrain(v)
            v.wait_ge(gp, 1)
            v.tensor_tensor(out=sb_const[:, :], in0=sb_rowi[:, :], in1=sb_coli[:, :],
                            op=ALU.is_equal)
            v.tensor_tensor(out=sb_u128[:, :], in0=sb_rowi[:, :], in1=sb_coli[:, :],
                            op=ALU.is_le)
            v.tensor_tensor(out=sb_esel[:, :], in0=sb_esel[:, :],
                            in1=sb_rowi[0:16, 0:1].broadcast_to([16, 1024]),
                            op=ALU.is_equal)
            v.tensor_scalar(out=sb_adder[:, :], in0=sb_iota8[:, :],
                            scalar1=float(REG), scalar2=-1.0, op0=ALU.mult, op1=ALU.add)
            v.drain().then_inc(dp, 2)

    _p1_guards = []

    def _scoped(name, shape, dtype=F32):
        g_ = nc.sbuf_tensor(name, shape, dtype)
        h_ = g_.__enter__()
        _p1_guards.append(g_)
        return h_.ap()

    sb_state = _scoped("p1_state", [128, 1024])
    sb_wq = _scoped("w_q", [128, 8 * 1024])
    sb_wk = _scoped("w_k", [128, 8 * 1024])
    sb_wv = _scoped("w_v", [128, 8 * 1024])
    sb_wsm = _scoped("w_sm", [128, 8 * 88])
    sb_wsmb = _scoped("w_smb", [1, 88])

    sb_xt = _scoped("p1_xt", [128, 2 * 1024])
    sb_ksb = _scoped("p1_ksb", [128, 1024])
    sb_vsb = _scoped("p1_vsb", [128, 1024])
    sb_vs2 = _scoped("p1_vs2", [128, 1024])
    sb_qsb = _scoped("p1_qsb", [128, 1024])
    sb_qaf = _scoped("p1_qaf", [128, 1024])
    sb_qas = _scoped("p1_qas", [128, 1024])
    sb_ot = _scoped("p1_ot", [128, 1024])
    sb_sig = _scoped("p1_sig", [128, 80])
    sb_stat = _scoped("p1_stat", [128, 96])
    sb_afas = _scoped("p1_afas", [128, 32])
    sb_afTf = _scoped("p1_afTf", [16, 128])
    sb_afTs = _scoped("p1_afTs", [16, 128])
    sb_g = _scoped("p1_g", [32, 1])
    sb_gT = _scoped("p1_gT", [1, 32])
    sb_gbc = _scoped("p1_gbc", [128, 16])
    sb_prod = _scoped("p1_prod", [128, 1024])
    sb_afe = _scoped("p1_afe", [128, 1024])
    sb_ase = _scoped("p1_ase", [128, 1024])

    ps_a = nc.alloc_psum_tensor("ps_a", [128, 512], F32).ap()
    ps_b = nc.alloc_psum_tensor("ps_b", [128, 1024], F32).ap()
    ps_c = nc.alloc_psum_tensor("ps_c", [128, 1024], F32).ap()
    ps_d = nc.alloc_psum_tensor("ps_d", [128, 1024], F32).ap()
    ps_x = nc.alloc_psum_tensor("ps_x", [128, 512], F32).ap()

    with nc.Block() as blk:
        @blk.sync
        def _(s):
            for w_hbm, w_sb, cols in ((wq, sb_wq, 1024), (wk, sb_wk, 1024),
                                      (wv, sb_wv, 1024)):
                s.dma_start(
                    out=w_sb[:, :].rearrange("p (i c) -> p i c", i=8),
                    in_=w_hbm[:, :].rearrange("(i p) c -> p i c", p=128),
                ).then_inc(sq, 16)
            s.dma_start(
                out=sb_wsm[:, :].rearrange("p (i c) -> p i c", i=8),
                in_=wsm[:, :].rearrange("(i p) c -> p i c", p=128),
            ).then_inc(sq, 16)
            s.dma_start(out=sb_wsmb[:, :], in_=wsmb[:, :]).then_inc(sq, 16)
            for i in range(NCH):
                if i >= 2:
                    s.wait_ge(tp, TPC * (i - 2) + 4)
                s.dma_start(out=sb_xt[:, (i % 2) * 1024:(i % 2 + 1) * 1024]
                            .rearrange("p (i c) -> p i c", i=8),
                            in_=xT[i, :, :].rearrange("(i p) c -> p i c", p=128)
                            ).then_inc(sx0 if i % 2 == 0 else sx1, 16)

        @blk.tensor
        def _(t):
            def mm(*args, inc=False, **kw):
                ins = t.matmul(*args, **kw)
                if inc:
                    ins.then_inc(tp, 1)
                return ins

            t.wait_ge(dp, 2)
            t.wait_ge(sq, 80)
            for i in range(NCH):
                bt, ba, bd = TPC * i, APC * i, DPC * i + 2
                xt = sb_xt[:, (i % 2) * 1024:(i % 2 + 1) * 1024]
                t.wait_ge(sx0 if i % 2 == 0 else sx1, 16 * (i // 2 + 1))
                if i > 0:
                    t.wait_ge(dp, DPC * i + 2)
                    t.wait_ge(ap_, APC * i)
                for ii in range(8):
                    mm(ps_a[:, 0:88], xt[:, ii * 128:(ii + 1) * 128],
                       sb_wsm[:, ii * 88:(ii + 1) * 88], start=(ii == 0), stop=False)
                mm(ps_a[:, 0:88], sb_ones_r[:, :], sb_wsmb[:, :],
                   start=False, stop=True, inc=True)
                for half in range(2):
                    for ii in range(8):
                        mm(ps_b[:, half * 512:(half + 1) * 512],
                           xt[:, ii * 128:(ii + 1) * 128],
                           sb_wk[:, ii * 1024 + half * 512:ii * 1024 + (half + 1) * 512],
                           start=(ii == 0), stop=(ii == 7),
                           inc=(half == 1 and ii == 7))
                for half in range(2):
                    for ii in range(8):
                        mm(ps_c[:, half * 512:(half + 1) * 512],
                           xt[:, ii * 128:(ii + 1) * 128],
                           sb_wv[:, ii * 1024 + half * 512:ii * 1024 + (half + 1) * 512],
                           start=(ii == 0), stop=(ii == 7),
                           inc=(half == 1 and ii == 7))
                for half in range(2):
                    for ii in range(8):
                        mm(ps_d[:, half * 512:(half + 1) * 512],
                           xt[:, ii * 128:(ii + 1) * 128],
                           sb_wq[:, ii * 1024 + half * 512:ii * 1024 + (half + 1) * 512],
                           start=(ii == 0), stop=(ii == 7),
                           inc=(half == 1 and ii == 7))
                t.wait_ge(ap_, ba + 4)
                for jt in range(8):
                    ins = t.transpose(ps_d[:, jt * 128:(jt + 1) * 128],
                                      sb_qsb[:, jt * 128:(jt + 1) * 128],
                                      sb_const[:, :])
                ins.then_inc(tp, 1)
                t.wait_ge(dp, bd + 4)
                t.transpose(ps_x[0:16, 0:128], sb_afas[:, 0:16], sb_const[:, :])
                t.transpose(ps_x[0:16, 128:256], sb_afas[:, 16:32], sb_const[:, :])
                mm(ps_x[0:32, 256:257], sb_stat[:, 64:96], sb_ones_c[:, :],
                   start=True, stop=True, inc=True)
                t.wait_ge(ap_, ba + 3)
                t.wait_ge(dp, bd + 5)
                for jt in range(8):
                    mm(ps_b[:, jt * 128:(jt + 1) * 128],
                       sb_esel[:, jt * 128:(jt + 1) * 128], sb_afTf[:, :],
                       start=(jt % 4 == 0), stop=(jt % 4 == 3))
                for jt in range(8):
                    mm(ps_c[:, jt * 128:(jt + 1) * 128],
                       sb_esel[:, jt * 128:(jt + 1) * 128], sb_afTs[:, :],
                       start=(jt % 4 == 0), stop=(jt % 4 == 3), inc=(jt == 7))
                t.wait_ge(dp, bd + 6)
                t.transpose(ps_x[0:1, 320:352], sb_g[:, 0:1],
                            sb_const[0:32, 0:32]).then_inc(tp, 1)
                t.wait_ge(dp, bd + 7)
                gT3 = sb_gT[:, :].rearrange("o (s hp odd) -> o s hp odd", s=2, hp=8)
                mm(ps_x[0:64, 384:400], sb_ones_r[:, 0:64],
                   gT3[:, :, :, 0].rearrange("o s hp -> o (s hp)"),
                   start=True, stop=True)
                mm(ps_x[64:128, 384:400], sb_ones_r[:, 0:64],
                   gT3[:, :, :, 1].rearrange("o s hp -> o (s hp)"),
                   start=True, stop=True, inc=True)
                t.wait_ge(dp, bd + 9)
                t.wait_ge(ap_, ba + 4)
                for sl in range(2):
                    vop = sb_vsb if sl == 0 else sb_vs2
                    for par in range(2):
                        for hp in range(8):
                            h = 2 * hp + par
                            pb = par * 64
                            col = (sl * 8 + hp) * 64
                            mm(ps_b[pb:pb + 64, col:col + 64],
                               vop[:, h * 64:(h + 1) * 64],
                               sb_ksb[:, h * 64:(h + 1) * 64],
                               start=(hp == 0), stop=(hp == 7),
                               inc=(sl == 1 and par == 1 and hp == 7))
                t.wait_ge(dp, bd + 11)
                for b in range(2):
                    for par in range(2):
                        for sl in range(2):
                            qop = sb_qaf if sl == 0 else sb_qas
                            for j in range(4):
                                hp = 4 * b + j
                                h = 2 * hp + par
                                pb = par * 64
                                col = hp * 128
                                mm(ps_c[pb:pb + 64, col:col + 128],
                                   sb_state[pb:pb + 64, (sl * 8 + hp) * 64:(sl * 8 + hp) * 64 + 64],
                                   qop[pb:pb + 64, hp * 128:(hp + 1) * 128],
                                   start=(sl == 0 and j == 0), stop=(sl == 1 and j == 3),
                                   inc=(b == 1 and par == 1 and sl == 1 and j == 3))

        @blk.scalar
        def _(sc):
            sc = AutoDrain(sc)
            sc.wait_ge(gp, 1)
            for i in range(NCH):
                for z in range(8 * i, 8 * (i + 1)):
                    sc.dma_start(out=bufd[z * 128:(z + 1) * 128, :],
                                 in_=sb_zero[:, :]).then_inc(szf, 16)
                bt, ba, bd = TPC * i, APC * i, DPC * i + 2
                sc.wait_ge(tp, bt + 1)
                if i > 0:
                    sc.wait_ge(dp, DPC * (i - 1) + 2 + 4)
                sc.activation(out=sb_sig[:, 0:80], in_=ps_a[:, 0:80],
                              func=AF.Sigmoid)
                sc.drain().then_inc(ap_, 1)
                sc.wait_ge(tp, bt + 2)
                if i > 0:
                    sc.wait_ge(tp, TPC * (i - 1) + 9)
                sc.activation(out=sb_ksb[:, :], in_=ps_b[:, :], func=AF.Silu)
                sc.drain().then_inc(ap_, 1)
                sc.wait_ge(tp, bt + 3)
                sc.activation(out=sb_vsb[:, :], in_=ps_c[:, :], func=AF.Silu)
                sc.drain().then_inc(ap_, 1)
                sc.wait_ge(tp, bt + 4)
                if i > 0:
                    sc.wait_ge(dp, DPC * (i - 1) + 2 + 9)
                sc.activation(out=sb_qsb[:, :], in_=ps_d[:, :], func=AF.Silu)
                sc.drain().then_inc(ap_, 1)
                sc.wait_ge(dp, bd + 2)
                sc.activation(out=sb_stat[:, 16:32], in_=sb_stat[:, 16:32], func=AF.Sqrt)
                sc.drain()
                sc.activation(out=sb_stat[:, 48:64], in_=sb_stat[:, 48:64], func=AF.Sqrt)
                sc.activation(out=sb_stat[:, 0:16], in_=sb_stat[:, 0:16],
                              func=AF.Abs, scale=3.0)
                sc.drain().then_inc(ap_, 1)
                sc.wait_ge(dp, bd + 3)
                sc.activation(out=sb_stat[:, 0:16], in_=sb_stat[:, 0:16], func=AF.Sigmoid)
                sc.drain()
                sc.activation(out=sb_stat[:, 48:64], in_=sb_stat[:, 0:16],
                              func=AF.Sigmoid, bias=sb_m5[:, :], scale=10.0)
                sc.drain().then_inc(ap_, 1)
                sc.wait_ge(dp, bd + 5)
                for _ in range(7):
                    sc.activation(out=sb_g[:, 0:1], in_=sb_g[:, 0:1], func=AF.Square)
                    sc.drain()
                sc.drain().then_inc(ap_, 1)

        @blk.vector
        def _(v):
            v = AutoDrain(v)
            for i in range(NCH):
                bt, ba, bd = TPC * i, APC * i, DPC * i + 2
                v.wait_ge(tp, bt + 1)
                v.tensor_copy(out=sb_lx0[:, i * 8:(i + 1) * 8], in_=ps_a[:, 80:88])
                v.drain().then_inc(dp, 1)
                v.wait_ge(ap_, ba + 3)
                bexp = sb_sig[:, 0:16].broadcast_to([128, 16, 64])
                k3 = sb_ksb[:, :].rearrange("p (h e) -> p h e", h=16)
                v3 = sb_vsb[:, :].rearrange("p (h e) -> p h e", h=16)
                v.tensor_tensor(out=k3, in0=k3, in1=bexp, op=ALU.mult)
                v.tensor_tensor(out=v3, in0=v3, in1=bexp, op=ALU.mult)
                v.drain()
                p3 = sb_prod[:, :].rearrange("p (h e) -> p h e", h=16)
                v.tensor_tensor(out=sb_prod[:, :], in0=sb_ksb[:, :], in1=sb_vsb[:, :],
                                op=ALU.mult)
                v.drain()
                v.reduce_sum(sb_stat[:, 0:16].rearrange("p (h o) -> p h o", o=1),
                             p3, axis=AX.X)
                v.drain()
                v.tensor_tensor(out=sb_prod[:, :], in0=sb_ksb[:, :], in1=sb_ksb[:, :],
                                op=ALU.mult)
                v.drain()
                v.reduce_sum(sb_stat[:, 16:32].rearrange("p (h o) -> p h o", o=1),
                             p3, axis=AX.X)
                v.drain()
                v.tensor_tensor(out=sb_prod[:, :], in0=sb_vsb[:, :], in1=sb_vsb[:, :],
                                op=ALU.mult)
                v.drain()
                v.reduce_sum(sb_stat[:, 48:64].rearrange("p (h o) -> p h o", o=1),
                             p3, axis=AX.X)
                v.drain().then_inc(dp, 1)
                v.wait_ge(ap_, ba + 5)
                v.tensor_scalar_add(sb_stat[:, 16:32], sb_stat[:, 16:32], 1e-8)
                v.tensor_scalar_add(sb_stat[:, 48:64], sb_stat[:, 48:64], 1e-8)
                v.drain()
                v.tensor_tensor(out=sb_stat[:, 16:32], in0=sb_stat[:, 16:32],
                                in1=sb_stat[:, 48:64], op=ALU.mult)
                v.drain()
                v.reciprocal(sb_stat[:, 32:48], sb_stat[:, 16:32])
                v.drain()
                v.tensor_tensor(out=sb_stat[:, 0:16], in0=sb_stat[:, 0:16],
                                in1=sb_stat[:, 32:48], op=ALU.mult)
                v.drain().then_inc(dp, 1)
                v.wait_ge(ap_, ba + 6)
                v.tensor_scalar(out=sb_afas[:, 0:16], in0=sb_stat[:, 0:16],
                                scalar1=0.3, scalar2=0.5, op0=ALU.mult, op1=ALU.add)
                v.drain()
                v.tensor_scalar(out=sb_afas[:, 16:32], in0=sb_afas[:, 0:16],
                                scalar1=-1.0, scalar2=1.0, op0=ALU.mult, op1=ALU.add)
                v.drain()
                v.tensor_tensor(out=sb_afas[:, 0:16], in0=sb_afas[:, 0:16],
                                in1=sb_sig[:, 48:64], op=ALU.mult)
                v.tensor_tensor(out=sb_afas[:, 16:32], in0=sb_afas[:, 16:32],
                                in1=sb_sig[:, 64:80], op=ALU.mult)
                v.tensor_scalar(out=sb_stat[:, 32:48], in0=sb_stat[:, 0:16],
                                scalar1=-0.1, scalar2=1.0, op0=ALU.mult, op1=ALU.add)
                v.drain()
                v.tensor_tensor(out=sb_stat[:, 64:80], in0=sb_sig[:, 16:32],
                                in1=sb_stat[:, 32:48], op=ALU.mult)
                v.tensor_scalar(out=sb_stat[:, 32:48], in0=sb_stat[:, 0:16],
                                scalar1=-0.05, scalar2=1.0, op0=ALU.mult, op1=ALU.add)
                v.drain()
                v.tensor_tensor(out=sb_stat[:, 80:96], in0=sb_sig[:, 32:48],
                                in1=sb_stat[:, 32:48], op=ALU.mult)
                v.tensor_tensor(out=sb_stat[:, 32:48], in0=sb_stat[:, 48:64],
                                in1=sb_stat[:, 48:64], op=ALU.mult)
                v.drain()
                v.tensor_tensor(out=sb_vs2[:, :].rearrange("p (h e) -> p h e", h=16),
                                in0=sb_vsb[:, :].rearrange("p (h e) -> p h e", h=16),
                                in1=sb_stat[:, 32:48].broadcast_to([128, 16, 64]),
                                op=ALU.mult)
                v.drain().then_inc(dp, 1)
                v.wait_ge(tp, bt + 6)
                v.tensor_copy(out=sb_afTf[:, :], in_=ps_x[0:16, 0:128])
                v.tensor_copy(out=sb_afTs[:, :], in_=ps_x[0:16, 128:256])
                v.tensor_scalar_mul(sb_g[:, 0:1], ps_x[0:32, 256:257], 1.0 / 128)
                v.drain().then_inc(dp, 1)
                v.wait_ge(ap_, ba + 7)
                v.tensor_copy(out=sb_gall[:, i:i + 1], in_=sb_g[:, 0:1])
                v.tensor_tensor(out=sb_gloc[:, i + 1:i + 2], in0=sb_gloc[:, i:i + 1],
                                in1=sb_g[:, 0:1], op=ALU.mult)
                v.drain().then_inc(dp, 1)
                v.wait_ge(tp, bt + 8)
                v.tensor_copy(out=sb_gT[:, :], in_=ps_x[0:1, 320:352])
                v.drain().then_inc(dp, 1)
                v.wait_ge(tp, bt + 9)
                v.tensor_copy(out=sb_gbc[:, :], in_=ps_x[:, 384:400])
                v.drain().then_inc(dp, 1)
                v.wait_ge(tp, bt + 7)
                v.wait_ge(ap_, ba + 4)
                if i > 0:
                    v.wait_ge(sspq, 32 * i)
                v.tensor_copy(out=sb_afe[:, :], in_=ps_b[:, :])
                v.tensor_copy(out=sb_ase[:, :], in_=ps_c[:, :])
                v.tensor_tensor(out=sb_qaf[:, :], in0=ps_d[:, :], in1=sb_afe[:, :],
                                op=ALU.mult)
                v.tensor_tensor(out=sb_qas[:, :], in0=ps_d[:, :], in1=sb_ase[:, :],
                                op=ALU.mult)
                v.drain().then_inc(dp, 1)
                v.wait_ge(tp, bt + 10)
                if i == 0:
                    v.wait_ge(gp, 2)
                v.tensor_tensor(out=sb_state[:, :].rearrange("p (cb e) -> p cb e", cb=16),
                                in0=sb_state[:, :].rearrange("p (cb e) -> p cb e", cb=16),
                                in1=sb_gbc[:, :].broadcast_to([128, 16, 64]),
                                op=ALU.mult)
                v.drain()
                v.drain().then_inc(dp, 1)
                v.tensor_tensor(out=sb_state[:, :], in0=sb_state[:, :],
                                in1=ps_b[:, :], op=ALU.add)
                v.drain()
                v.drain().then_inc(dp, 1)
                v.wait_ge(tp, bt + 11)
                if i > 0:
                    v.wait_ge(sspo, 16 * i)
                v.tensor_copy(out=sb_ot[:, :], in_=ps_c[:, :])
                v.drain().then_inc(dp, 1)

        @blk.gpsimd
        def _(g):
            g.memset(sb_state[:, :], 0.0)
            g.drain().then_inc(gp, 1)
            for i in range(NCH):
                bd = DPC * i + 2
                g.wait_ge(dp, bd + 9)
                g.dma_start(out=qfsp[i, :, :].rearrange("(jt p) t -> p jt t", p=128),
                            in_=sb_qaf[:, :].rearrange("p (jt t) -> p jt t", jt=8)
                            ).then_inc(sspq, 16)
                g.dma_start(out=qssp[i, :, :].rearrange("(jt p) t -> p jt t", p=128),
                            in_=sb_qas[:, :].rearrange("p (jt t) -> p jt t", jt=8)
                            ).then_inc(sspq, 16)
                g.wait_ge(dp, bd + 12)
                g.dma_start(out=otsp[i, :, :].rearrange("(jt p) t -> p jt t", p=128),
                            in_=sb_ot[:, :].rearrange("p (jt t) -> p jt t", jt=8)
                            ).then_inc(sspo, 16)

    with nc.Block() as blk:
        @blk.gpsimd
        def _(g):
            g.wait_ge(dp, 2 + DPC * NCH)
            g.wait_ge(sspq, 32 * NCH)
            g.wait_ge(sspo, 16 * NCH)
            for odd in range(2):
                g.dma_start(
                    out=AP(tensor=cc1i.tensor, offset=cc1i.offset + odd * 4097,
                           ap=[[64, 64], [16 * 4097, 2], [2 * 4097, 8], [1, 64]]),
                    in_=sb_state[odd * 64:(odd + 1) * 64, :]
                    .rearrange("p (s hp e) -> p s hp e", s=2, hp=8),
                ).then_inc(scw, 16)
            g.dma_start(out=cc1i[:, 4096:4097], in_=sb_gloc[:, NCH:NCH + 1],
                        allow_slow_non_contiguous=True).then_inc(scw, 16)
            g.wait_ge(scw, 48)
            g.collective_compute(
                "AllGather", ALU.bypass, replica_groups=[list(range(NCORE))],
                ins=[cc1i.tensor.ap().opt()], outs=[cc1o.tensor.ap().opt()],
            ).then_inc(ccs, 1)
            g.wait_ge(ccs, 1)

        if debug:
            @blk.scalar
            def _(sc):
                sc.wait_ge(ccs, 1)
                sc.dma_start(out=dbg_state[:, :], in_=sb_state[:, :]).then_inc(sq, 16)
                sc.dma_start(out=dbg_gall[:, :], in_=sb_gall[:, :]).then_inc(sq, 16)
                sc.dma_start(out=dbg_gloc[:, :], in_=sb_gloc[:, :]).then_inc(sq, 16)
                sc.dma_start(out=dbg_lx0[:, :], in_=sb_lx0[:, :]).then_inc(sq, 16)
                sc.wait_ge(sq, 80 + 64)
                for half in range(2):
                    sc.dma_start(out=sb_wq[:, 0:4097],
                                 in_=cc1o[half * 128:(half + 1) * 128, :]).then_inc(sq, 16)
                    sc.wait_ge(sq, 160 + 32 * half)
                    sc.dma_start(out=dbg_cc1o[half * 128:(half + 1) * 128, :],
                                 in_=sb_wq[:, 0:4097]).then_inc(sq, 16)
                    sc.wait_ge(sq, 176 + 32 * half)

    for g_ in reversed(_p1_guards):
        g_.__exit__(None, None, None)

    _p3_guards = []

    def _scp3(name, shape, dtype=F32):
        g_ = nc.sbuf_tensor(name, shape, dtype)
        h_ = g_.__enter__()
        _p3_guards.append(g_)
        return h_.ap()

    sb_wo = _scp3("p3_wo", [128, 8 * 1024])
    sb_x1 = _scp3("p3_x1", [128, NCH * RW])
    sb_wor = _scp3("p3_wor", [128, 64])
    sb_sbc = _scp3("p3_sbc", [128, 8])
    sb_rt = _scp3("p3_rt", [128, 64])

    with nc.Block() as blk:
        @blk.sync
        def _(s):
            s.wait_ge(ccs, 1)
            s.dma_start(out=sb_wo[:, :].rearrange("p (i c) -> p i c", i=8),
                        in_=woT[:, :].rearrange("(i p) c -> p i c", p=128)
                        ).then_inc(syc0, 16)
            s.dma_start(out=sb_rt[:, :].rearrange("p (i c) -> p i c", i=8),
                        in_=rT[:, :].rearrange("(i p) c -> p i c", p=128)
                        ).then_inc(syc0, 16)

        @blk.tensor
        def _(t):
            t.wait_ge(syc0, 32)
            for jt in range(8):
                for ii in range(8):
                    t.matmul(ps_a[:, jt * 8:(jt + 1) * 8],
                             sb_wo[:, ii * 1024 + jt * 128:ii * 1024 + (jt + 1) * 128],
                             sb_rt[:, ii * 8:(ii + 1) * 8],
                             start=(jt % 8 == 0 and ii == 0),
                             stop=(jt == 7 and ii == 7))
            for ii in range(8):
                t.matmul(ps_x[0:1, 0:8], sb_ones_c[:, :], sb_rt[:, ii * 8:(ii + 1) * 8],
                         start=(ii == 0), stop=(ii == 7)).then_inc(set_, 1) if ii == 7 else \
                    t.matmul(ps_x[0:1, 0:8], sb_ones_c[:, :], sb_rt[:, ii * 8:(ii + 1) * 8],
                             start=(ii == 0), stop=(ii == 7))

        @blk.vector
        def _(v):
            v = AutoDrain(v)
            v.wait_ge(set_, 1)
            v.tensor_copy(out=sb_wor[:, :], in_=ps_a[:, 0:64])
            v.tensor_copy(out=sb_rt[0:1, 0:8], in_=ps_x[0:1, 0:8])
            v.drain().then_inc(sed, 1)

        @blk.tensor
        def _(t):
            t.wait_ge(sed, 1)
            t.matmul(ps_x[:, 8:16], sb_ones_r[:, :], sb_rt[0:1, 0:8],
                     start=True, stop=True).then_inc(set_, 1)

        @blk.vector
        def _(v):
            v = AutoDrain(v)
            v.wait_ge(set_, 2)
            v.tensor_copy(out=sb_sbc[:, :], in_=ps_x[:, 8:16])
            v.drain().then_inc(sed, 1)

        @blk.sync
        def _(s):
            s.wait_ge(set_, 2)
            s.dma_start(out=sb_wo[:, :].rearrange("p (i c) -> p i c", i=8),
                        in_=wo[:, :].rearrange("(i p) c -> p i c", p=128)
                        ).then_inc(syc0, 16)

    _p3a_guards = []

    def _sc3(name, shape, dtype=F32):
        g_ = nc.sbuf_tensor(name, shape, dtype)
        h_ = g_.__enter__()
        _p3a_guards.append(g_)
        return h_.ap()

    sb_sent = _sc3("p3_sent", [32, 4096])
    sb_sj = _sc3("p3_sj", [32, 4096])
    sb_gj = _sc3("p3_gj", [32, 8])
    sb_al = _sc3("p3_al", [32, 8])
    sb_alt = _sc3("p3_alt", [32, 8])
    sb_segp = _sc3("p3_segp", [32, 8])
    sb_sentT = _sc3("p3_sentT", [128, 1024])
    sb_qgf = _sc3("p3_qgf", [128, 2048])
    sb_qgs = _sc3("p3_qgs", [128, 2048])
    sb_glP0 = _sc3("p3_glP0", [1, 512])
    sb_glbc = _sc3("p3_glbc", [128, 32])
    sb_corr = _sc3("p3_corr", [128, 2048])

    with nc.Block() as blk:
        @blk.gpsimd
        def _(g):
            g = AutoDrain(g)
            g.wait_ge(ccs, 1)
            g.memset(sb_x1[:, :], 0.0)
            g.memset(sb_sent[:, :], 0.0)
            g.dma_start(out=sb_segp[:, :], in_=segp[:, :]).then_inc(sg2, 16)
            g.dma_start(out=sb_gj[:, :],
                        in_=AP(tensor=cc1o.tensor, offset=cc1o.offset + 4096,
                               ap=[[4097, 32], [32 * 4097, 8]]),
                        allow_slow_non_contiguous=True).then_inc(sg2, 16)
            g.dma_start(out=sb_glP0[:, :],
                        in_=sb_gloc[:, 1:NCH + 1],
                        allow_slow_non_contiguous=True).then_inc(sg2, 16)
            for j in range(NCORE):
                g.wait_ge(sed, 2 + 2 * j)
                g.dma_start(out=sb_sj[:, :],
                            in_=cc1o[32 * j:32 * (j + 1), 0:4096]).then_inc(ssj, 16)

        @blk.vector
        def _(v):
            v = AutoDrain(v)
            v.wait_ge(sg2, 48)
            v.tensor_tensor(out=sb_al[:, :], in0=sb_gj[:, :], in1=sb_segp[:, :],
                            op=ALU.mult)
            v.tensor_tensor(out=sb_al[:, :], in0=sb_al[:, :], in1=sb_segp[:, :],
                            op=ALU.subtract)
            v.tensor_scalar_add(sb_al[:, :], sb_al[:, :], 1.0)
            for j in range(NCORE):
                v.wait_ge(ssj, 16 * (j + 1))
                v.tensor_scalar(out=sb_sj[:, :], in0=sb_sj[:, :],
                                scalar1=sb_segp[:, j:j + 1], scalar2=None, op0=ALU.mult)
                v.scalar_tensor_tensor(out=sb_sent[:, :], in0=sb_sent[:, :],
                                       scalar=sb_al[:, j:j + 1], in1=sb_sj[:, :],
                                       op0=ALU.mult, op1=ALU.add)
                v.drain().then_inc(sed, 2)

        @blk.gpsimd
        def _(g):
            g.wait_ge(sed, 2 + 2 * NCORE)
            g.dma_start(out=sentb[:, :], in_=sb_sent[:, :]).then_inc(ssb, 16)
            g.wait_ge(ssb, 16)
            for par in range(2):
                g.dma_start(
                    out=sb_sentT[par * 64:(par + 1) * 64, :]
                    .rearrange("p (s hp e) -> p s hp e", s=2, hp=8),
                    in_=AP(tensor=sentb.tensor, offset=sentb.offset + par * 4096,
                           ap=[[64, 64], [16 * 4096, 2], [2 * 4096, 8], [1, 64]]),
                ).then_inc(ssb, 16)

    SEDB = 18
    with nc.Block() as blk:
        @blk.sync
        def _(s):
            for hp in range(8):
                if hp > 0:
                    s.wait_ge(set_, 2 + 2 * hp)
                s.dma_start(out=sb_qgf[:, :].rearrange("p (c t) -> p c t", c=16),
                            in_=qfsp[:, 128 * hp:128 * (hp + 1), :]
                            .rearrange("c p t -> p c t")).then_inc(syc1, 16)
                s.dma_start(out=sb_qgs[:, :].rearrange("p (c t) -> p c t", c=16),
                            in_=qssp[:, 128 * hp:128 * (hp + 1), :]
                            .rearrange("c p t -> p c t")).then_inc(syc1, 16)

        @blk.tensor
        def _(t):
            t.wait_ge(ssb, 48)
            for hp in range(8):
                rows = [2 * hp, 2 * hp + 1, 16 + 2 * hp, 16 + 2 * hp + 1]
                if hp > 0:
                    t.wait_ge(sed, SEDB + 3 * (hp - 1) + 1)
                for k, (pb, scol) in enumerate(((0, 0), (64, 0), (0, 16), (64, 16))):
                    ins = t.matmul(ps_x[pb:pb + 64, 32 + scol:48 + scol],
                                   sb_ones_r[:, 0:64],
                                   sb_glP0[:, rows[k] * 16:(rows[k] + 1) * 16],
                                   start=True, stop=True)
                    if k == 3:
                        ins.then_inc(set_, 1)
                t.wait_ge(sed, SEDB + 3 * hp + 2)
                for par in range(2):
                    pb = par * 64
                    for sl in range(2):
                        qg = sb_qgf if sl == 0 else sb_qgs
                        for tq in range(4):
                            tgt = ps_b if tq < 2 else ps_c
                            col = (tq % 2) * 512
                            ins = t.matmul(
                                tgt[pb:pb + 64, col:col + 512],
                                sb_sentT[pb:pb + 64,
                                         (sl * 8 + hp) * 64:(sl * 8 + hp) * 64 + 64],
                                qg[pb:pb + 64, tq * 512:(tq + 1) * 512],
                                start=(sl == 0), stop=(sl == 1))
                            if par == 1 and sl == 1 and tq == 3:
                                ins.then_inc(set_, 1)

        @blk.vector
        def _(v):
            v = AutoDrain(v)
            for hp in range(8):
                v.wait_ge(set_, 2 + 2 * hp + 1)
                v.wait_ge(syc1, 32 * (hp + 1))
                v.tensor_copy(out=sb_glbc[:, :], in_=ps_x[:, 32:64])
                v.drain().then_inc(sed, 1)
                v.tensor_tensor(out=sb_qgf[:, :].rearrange("p (c t) -> p c t", c=16),
                                in0=sb_qgf[:, :].rearrange("p (c t) -> p c t", c=16),
                                in1=sb_glbc[:, 0:16].broadcast_to([128, 16, 128]),
                                op=ALU.mult)
                v.tensor_tensor(out=sb_qgs[:, :].rearrange("p (c t) -> p c t", c=16),
                                in0=sb_qgs[:, :].rearrange("p (c t) -> p c t", c=16),
                                in1=sb_glbc[:, 16:32].broadcast_to([128, 16, 128]),
                                op=ALU.mult)
                v.drain().then_inc(sed, 1)
                v.wait_ge(set_, 2 + 2 * hp + 2)
                if hp > 0:
                    v.wait_ge(sco, 16 * hp)
                v.tensor_copy(out=sb_corr[:, 0:1024], in_=ps_b[:, :])
                v.tensor_copy(out=sb_corr[:, 1024:2048], in_=ps_c[:, :])
                v.drain().then_inc(sed, 1)

        @blk.gpsimd
        def _(g):
            g.wait_ge(sspo, 16 * NCH)
            for hp in range(8):
                g.wait_ge(sed, SEDB + 3 * hp + 3)
                g.dma_start(out=otsp[:, 128 * hp:128 * (hp + 1), :]
                            .rearrange("c p t -> p c t"),
                            in_=sb_corr[:, :].rearrange("p (c t) -> p c t", c=16),
                            accum_op=ALU.add).then_inc(sco, 16)

    for g_ in reversed(_p3a_guards):
        g_.__exit__(None, None, None)

    _p3b_guards = []

    def _scb(name, shape, dtype=F32):
        g_ = nc.sbuf_tensor(name, shape, dtype)
        h_ = g_.__enter__()
        _p3b_guards.append(g_)
        return h_.ap()

    sb_otc = _scb("b_otc", [128, 2 * 1024])
    sb_x0c = _scb("b_x0c", [128, 2 * 1024])
    sb_xp = _scb("b_xp", [128, 1024])
    sb_sq = _scb("b_sq", [128, 1024])
    sb_rs = _scb("b_rs", [128, 16])
    sb_l8 = _scb("b_l8", [128, 8])
    sb_l8b = _scb("b_l8b", [128, 8])
    sb_ex = _scb("b_ex", [128, 8])
    sb_ohw = _scb("b_ohw", [128, 8])
    sb_oh = _scb("b_oh", [128, 128])
    sb_dest = _scb("b_dest", [128, NCH], I32)
    sb_cnt8 = _scb("b_cnt8", [8, 1])
    sb_cc2 = _scb("b_cc2", [8, 8])
    sb_cntp = _scb("b_cntp", [8, 1])
    sb_thr8 = _scb("b_thr8", [8, 1])
    sb_throw = _scb("b_throw", [1, 8])

    SB3 = 42
    TB3 = 18
    SY0B, SY1B = 48, 256

    with nc.Block() as blk:
        @blk.sync
        def _(s):
            s.wait_ge(sco, 128)
            for tau in range(NCH):
                sem = syc0 if tau % 2 == 0 else syc1
                base = SY0B if tau % 2 == 0 else SY1B
                if tau >= 2:
                    s.wait_ge(set_, TB3 + 2 * (tau - 2) + 1)
                    s.wait_ge(sed, SB3 + 5 * (tau - 2) + 3)
                half = (tau % 2) * 1024
                s.dma_start(out=sb_otc[:, half:half + 1024]
                            .rearrange("p (j t) -> p j t", j=8),
                            in_=otsp[tau, :, :].rearrange("(j p) t -> p j t", p=128)
                            ).then_inc(sem, 16)
                s.dma_start(out=sb_x0c[:, half:half + 1024],
                            in_=x0r[tau * 128:(tau + 1) * 128, :]).then_inc(sem, 16)

        @blk.tensor
        def _(t):
            for tau in range(NCH):
                sem = syc0 if tau % 2 == 0 else syc1
                base = SY0B if tau % 2 == 0 else SY1B
                half = (tau % 2) * 1024
                t.wait_ge(sem, base + 32 * (tau // 2 + 1))
                if tau > 0:
                    t.wait_ge(sed, SB3 + 5 * (tau - 1) + 3)
                for jt in range(8):
                    ot_t = sb_otc[:, half + jt * 128:half + (jt + 1) * 128]
                    for hh in range(2):
                        t.matmul(ps_d[:, hh * 512:(hh + 1) * 512], ot_t,
                                 sb_wo[:, jt * 1024 + hh * 512:jt * 1024 + (hh + 1) * 512],
                                 start=(jt == 0), stop=(jt == 7))
                    ins = t.matmul(ps_a[:, 0:8], ot_t, sb_wor[:, jt * 8:(jt + 1) * 8],
                                   start=(jt == 0), stop=(jt == 7))
                    if jt == 7:
                        ins.then_inc(set_, 1)
                t.wait_ge(sed, SB3 + 5 * tau + 4)
                for tp_ in range(tau + 1):
                    lhs = sb_u128 if tp_ == tau else sb_o128
                    t.matmul(ps_a[:, 128:136], lhs,
                             sb_oh[:, tp_ * 8:(tp_ + 1) * 8],
                             start=(tp_ == 0), stop=(tp_ == tau))
                ins = t.matmul(ps_x[0:8, 0:1], sb_oh[:, tau * 8:(tau + 1) * 8],
                               sb_ones_c[:, :], start=(tau == 0), stop=(tau == NCH - 1),
                               skip_group_check=True)
                ins.then_inc(set_, 1)

        @blk.scalar
        def _(sc):
            sc = AutoDrain(sc)
            for tau in range(NCH):
                AB = 3 * tau
                sc.wait_ge(sed, SB3 + 5 * tau + 1)
                sc.activation(out=sb_sq[:, :], in_=sb_xp[:, :], func=AF.Square,
                              accum_out=sb_rs[:, 1:2])
                sc.drain().then_inc(sea, 1)
                sc.wait_ge(sed, SB3 + 5 * tau + 2)
                sc.activation(out=sb_rs[:, 4:5], in_=sb_rs[:, 3:4], func=AF.Sqrt,
                              bias=sb_eps[:, :])
                sc.drain().then_inc(sea, 1)
                sc.wait_ge(sed, SB3 + 5 * tau + 3)
                sc.activation(out=sb_ex[:, :], in_=sb_l8[:, :], func=AF.Exp,
                              bias=sb_rs[:, 7:8], accum_out=sb_rs[:, 8:9])
                sc.drain().then_inc(sea, 1)

        @blk.vector
        def _(v):
            v = AutoDrain(v)
            for tau in range(NCH):
                sem = syc0 if tau % 2 == 0 else syc1
                base = SY0B if tau % 2 == 0 else SY1B
                half = (tau % 2) * 1024
                AB = 3 * tau
                v.wait_ge(set_, TB3 + 2 * tau + 1)
                v.wait_ge(sem, base + 32 * (tau // 2 + 1))
                if tau > 0:
                    v.wait_ge(sea, AB - 3 + 1)
                v.tensor_tensor(out=sb_xp[:, :], in0=ps_d[:, :],
                                in1=sb_x0c[:, half:half + 1024], op=ALU.add)
                v.reduce_sum(sb_rs[:, 0:1], sb_xp[:, :], axis=AX.X)
                v.tensor_scalar_mul(sb_rs[:, 0:1], sb_rs[:, 0:1], 1.0 / 1024)
                v.drain().then_inc(sed, 1)
                v.wait_ge(sea, AB + 1)
                v.tensor_tensor(out=sb_rs[:, 2:3], in0=sb_rs[:, 0:1],
                                in1=sb_rs[:, 0:1], op=ALU.mult)
                v.tensor_scalar_mul(sb_rs[:, 3:4], sb_rs[:, 1:2], 1.0 / 1024)
                v.tensor_tensor(out=sb_rs[:, 3:4], in0=sb_rs[:, 3:4],
                                in1=sb_rs[:, 2:3], op=ALU.subtract)
                v.drain().then_inc(sed, 1)
                v.wait_ge(sea, AB + 2)
                v.reciprocal(sb_rs[:, 5:6], sb_rs[:, 4:5])
                v.tensor_scalar(out=sb_x1[:, tau * RW:tau * RW + 1024], in0=sb_xp[:, :],
                                scalar1=sb_rs[:, 0:1], scalar2=sb_rs[:, 5:6],
                                op0=ALU.subtract, op1=ALU.mult)
                v.tensor_tensor(out=sb_l8[:, :], in0=ps_a[:, 0:8],
                                in1=sb_lx0[:, tau * 8:(tau + 1) * 8], op=ALU.add)
                v.tensor_scalar(out=sb_l8b[:, :], in0=sb_sbc[:, :],
                                scalar1=sb_rs[:, 0:1], scalar2=None, op0=ALU.mult)
                v.tensor_tensor(out=sb_l8[:, :], in0=sb_l8[:, :], in1=sb_l8b[:, :],
                                op=ALU.subtract)
                v.tensor_scalar(out=sb_l8[:, :], in0=sb_l8[:, :],
                                scalar1=sb_rs[:, 5:6], scalar2=None, op0=ALU.mult)
                v.reduce_max(sb_rs[:, 6:7], sb_l8[:, :], axis=AX.X)
                v.tensor_scalar_mul(sb_rs[:, 7:8], sb_rs[:, 6:7], -1.0)
                v.drain().then_inc(sed, 1)
                v.wait_ge(sea, AB + 3)
                v.reciprocal(sb_rs[:, 9:10], sb_rs[:, 8:9])
                v.tensor_scalar(out=sb_ex[:, :], in0=sb_ex[:, :],
                                scalar1=sb_rs[:, 9:10], scalar2=None, op0=ALU.mult)
                v.reduce_max(sb_rs[:, 10:11], sb_ex[:, :], axis=AX.X)
                v.tensor_copy(out=sb_x1[:, tau * RW + 1024:tau * RW + 1025],
                              in_=sb_rs[:, 10:11])
                v.tensor_scalar_add(sb_x1[:, tau * RW + 1025:tau * RW + 1026],
                                    sb_iota_c[:, :], float(tau * 128))
                v.tensor_scalar(out=sb_ohw[:, :], in0=sb_ex[:, :],
                                scalar1=sb_rs[:, 10:11], scalar2=None, op0=ALU.is_equal)
                v.scalar_tensor_tensor(out=sb_ohw[:, :], in0=sb_ohw[:, :],
                                       scalar=-16.0, in1=sb_iota8[:, :],
                                       op0=ALU.mult, op1=ALU.add)
                v.tensor_reduce(sb_rs[:, 11:12], sb_ohw[:, :], axis=AX.X, op=ALU.min)
                v.tensor_scalar_add(sb_rs[:, 11:12], sb_rs[:, 11:12], 16.0)
                v.tensor_scalar(out=sb_oh[:, tau * 8:(tau + 1) * 8], in0=sb_iota8[:, :],
                                scalar1=sb_rs[:, 11:12], scalar2=None, op0=ALU.is_equal)
                v.drain().then_inc(sed, 1)
                v.wait_ge(set_, TB3 + 2 * tau + 2)
                v.tensor_tensor(out=sb_ohw[:, :], in0=ps_a[:, 128:136],
                                in1=sb_adder[:, :], op=ALU.add)
                v.tensor_tensor(out=sb_ohw[:, :], in0=sb_ohw[:, :],
                                in1=sb_oh[:, tau * 8:(tau + 1) * 8], op=ALU.mult)
                v.reduce_sum(sb_rs[:, 12:13], sb_ohw[:, :], axis=AX.X)
                v.tensor_copy(out=sb_dest[:, tau:tau + 1], in_=sb_rs[:, 12:13])
                v.drain().then_inc(sed, 1)

        @blk.gpsimd
        def _(g):
            g.wait_ge(szf, 16 * (E * REG // 128))
            for tau in range(NCH):
                g.wait_ge(sed, SB3 + 5 * tau + 5)
                g.indirect_dma_start(
                    out=bufd[:, :],
                    out_offset=IndirectOffsetOnAxis(sb_dest[:, tau:tau + 1], 0),
                    in_=sb_x1[:, tau * RW:(tau + 1) * RW], in_offset=None,
                ).then_inc(ssc, 16)

    SB4 = SB3 + 5 * NCH
    TB4 = TB3 + 2 * NCH
    with nc.Block() as blk:
        @blk.vector
        def _(v):
            v = AutoDrain(v)
            v.wait_ge(set_, TB4)
            v.tensor_copy(out=sb_cnt8[:, :], in_=ps_x[0:8, 0:1])
            v.drain().then_inc(sed, 1)
            v.wait_ge(set_, TB4 + 1)
            v.tensor_copy(out=sb_cntrow[:, :], in_=ps_x[0:1, 8:16])
            v.drain().then_inc(sed, 1)
            v.wait_ge(set_, TB4 + 2)
            v.tensor_scalar(out=sb_thr8[:, :], in0=ps_x[0:8, 16:17],
                            scalar1=-1.0, scalar2=float(CAP), op0=ALU.mult, op1=ALU.add)
            v.drain().then_inc(sed, 1)
            v.wait_ge(set_, TB4 + 3)
            v.tensor_copy(out=sb_throw[:, :], in_=ps_x[0:1, 24:32])
            v.drain().then_inc(sed, 1)
            v.wait_ge(set_, TB4 + 4)
            v.tensor_copy(out=sb_thrbc[:, :], in_=ps_b[:, 0:8])
            v.tensor_copy(out=sb_cntbc[:, :], in_=ps_b[:, 8:16])
            v.tensor_copy(out=sb_cti[:, :], in_=sb_cntrow[:, :])
            v.tensor_scalar(out=sb_cct[:, :], in0=sb_cti[:, :], scalar1=9,
                            scalar2=None, op0=ALU.arith_shift_right)
            v.tensor_scalar(out=sb_ctt[:, :], in0=sb_cct[:, :], scalar1=9,
                            scalar2=None, op0=ALU.arith_shift_left)
            v.tensor_tensor(out=sb_ctt[:, :], in0=sb_cti[:, :], in1=sb_ctt[:, :],
                            op=ALU.subtract)
            v.tensor_scalar_add(sb_ctt[:, :], sb_ctt[:, :], 127)
            v.tensor_scalar(out=sb_ctt[:, :], in0=sb_ctt[:, :], scalar1=7,
                            scalar2=None, op0=ALU.arith_shift_right)
            v.drain().then_inc(sed, 1)
            v.tensor_copy(out=sb_nff[:, :], in_=sb_cct[:, :])
            v.tensor_scalar(
                out=sb_sbrow[:, 0:32].rearrange("o (e t) -> o e t", e=8),
                in0=sb_nff[:, :].rearrange("o (e t) -> o e t", t=1)
                .broadcast_to([1, 8, 4]),
                scalar1=512.0, scalar2=None, op0=ALU.mult)
            v.tensor_tensor(out=sb_sbrow[:, 0:32], in0=sb_sbrow[:, 0:32],
                            in1=sb_io4r[:, :], op=ALU.add)
            v.drain()
            v.tensor_tensor(out=sb_sbrow[:, 32:64], in0=sb_sbrow[:, 0:32],
                            in1=sb_e2k[:, :], op=ALU.add)
            v.drain().then_inc(sed, 1)
            v.wait_ge(set_, TB4 + 5)
            v.tensor_copy(out=sb_sbase[:, :], in_=ps_b[:, 16:80])
            v.tensor_scalar(out=sb_tidx[:, :], in0=sb_sbase[:, 32:64],
                            scalar1=sb_iota_c[:, :], scalar2=None, op0=ALU.add)
            v.drain().then_inc(sed, 1)

        @blk.tensor
        def _(t):
            t.wait_ge(sed, SB4 + 1)
            t.transpose(ps_x[0:1, 8:16], sb_cnt8[:, :], sb_const[0:8, 0:8]
                        ).then_inc(set_, 1)
            t.wait_ge(s2o, 32)
            t.wait_ge(ccs, 2)
            t.matmul(ps_x[0:8, 16:17], sb_cc2[:, :], sb_cntp[:, :],
                     start=True, stop=True).then_inc(set_, 1)
            t.wait_ge(sed, SB4 + 3)
            t.transpose(ps_x[0:1, 24:32], sb_thr8[:, :], sb_const[0:8, 0:8]
                        ).then_inc(set_, 1)
            t.wait_ge(sed, SB4 + 4)
            t.matmul(ps_b[:, 0:8], sb_ones_r[:, :], sb_throw[:, :],
                     start=True, stop=True)
            t.matmul(ps_b[:, 8:16], sb_ones_r[:, :], sb_cntrow[:, :],
                     start=True, stop=True).then_inc(set_, 1)
            t.wait_ge(sed, SB4 + 6)
            t.matmul(ps_b[:, 16:80], sb_ones_r[:, :], sb_sbrow[:, :],
                     start=True, stop=True).then_inc(set_, 1)

        @blk.gpsimd
        def _(g):
            g.wait_ge(sed, SB4 + 2)
            g.dma_start(out=cc2i[:, :], in_=sb_cntrow[:, :]).then_inc(s2i, 16)
            g.wait_ge(s2i, 16)
            g.collective_compute(
                "AllGather", ALU.bypass, replica_groups=[list(range(NCORE))],
                ins=[cc2i.tensor.ap().opt()], outs=[cc2o.tensor.ap().opt()],
            ).then_inc(ccs, 1)
            g.wait_ge(ccs, 2)
            g.dma_start(out=sb_cc2[:, :], in_=cc2o[:, :]).then_inc(s2o, 16)
            g.dma_start(out=sb_cntp[:, :], in_=cntp[:, :]).then_inc(s2o, 16)

    if debug:
        with nc.Block() as blk:
            @blk.scalar
            def _(sc):
                sc.wait_ge(sed, SB4 + 5)
                sc.wait_ge(ssc, 16 * NCH)
                sc.dma_start(out=dbg_x1[:, :], in_=sb_x1[:, :]).then_inc(sq, 16)
                sc.dma_start(out=dbg_oh[:, :], in_=sb_oh[:, :]).then_inc(sq, 16)
                sc.dma_start(out=dbg_dest[:, :], in_=sb_dest[:, :]).then_inc(sq, 16)
                sc.dma_start(out=dbg_rows[0:1, 0:8], in_=sb_cntrow[:, :]).then_inc(sq, 16)
                sc.dma_start(out=dbg_rows[1:2, 0:8], in_=sb_throw[:, :]).then_inc(sq, 16)
                sc.wait_ge(sq, 208 + 80)

    for g_ in reversed(_p3b_guards):
        g_.__exit__(None, None, None)
    for g_ in reversed(_p3_guards):
        g_.__exit__(None, None, None)

    sb_w1b = nc.alloc_sbuf_tensor("p5_w1b", [128, 2 * 4096], F32).ap()
    sb_w2b = nc.alloc_sbuf_tensor("p5_w2b", [128, 2 * 4096], F32).ap()
    sb_hT = nc.alloc_sbuf_tensor("p5_hT", [128, 16384], F32).ap()
    sb_bfin = nc.alloc_sbuf_tensor("p5_bfin", [128, 4 * RW], F32).ap()
    sb_xtf = nc.alloc_sbuf_tensor("p5_xtf", [128, 4096], F32).ap()
    sb_eoT = nc.alloc_sbuf_tensor("p5_eoT", [128, 4096], F32).ap()
    sb_wb = nc.alloc_sbuf_tensor("p5_wb", [128, 1024], F32).ap()
    sb_wb2 = nc.alloc_sbuf_tensor("p5_wb2", [128, 1024], F32).ap()
    sb_sq2 = nc.alloc_sbuf_tensor("p5_sq2", [128, 1024], F32).ap()
    sb_b1e = nc.alloc_sbuf_tensor("p5_b1e", [128, 32], F32).ap()
    sb_b2e = nc.alloc_sbuf_tensor("p5_b2e", [128, 8], F32).ap()
    sb_r5 = nc.alloc_sbuf_tensor("p5_r5", [128, 16], F32).ap()
    sb_dsw = nc.alloc_sbuf_tensor("p5_dsw", [128, 1], I32).ap()

    sw1a = nc.alloc_semaphore("sw1a")
    sw1b = nc.alloc_semaphore("sw1b")
    sw2a = nc.alloc_semaphore("sw2a")
    sw2b = nc.alloc_semaphore("sw2b")
    sbf = nc.alloc_semaphore("sbf")
    sbf2 = nc.alloc_semaphore("sbf2")
    sbe = nc.alloc_semaphore("sbe")
    st5 = nc.alloc_semaphore("st5")
    sd5 = nc.alloc_semaphore("sd5")
    sa5 = nc.alloc_semaphore("sa5")
    swb = nc.alloc_semaphore("swb")
    stl1p = [nc.alloc_semaphore("stl1a"), nc.alloc_semaphore("stl1b")]
    stl2p = [nc.alloc_semaphore("stl2a"), nc.alloc_semaphore("stl2b")]
    sfin = nc.alloc_semaphore("sfin")
    sfin2 = nc.alloc_semaphore("sfin2")

    BLOCKS = []
    for e_ in range(E):
        for cc_ in range(4):
            BLOCKS.append(("F", e_, cc_))
        for tt_ in range(4):
            BLOCKS.append(("T", e_, tt_))
    NBLK = len(BLOCKS)
    S_PE = {"F": 24, "T": 18}
    S_DV = {"F": 17, "T": 5}
    S_AC = {"F": 16, "T": 10}
    NSC = {"F": 4, "T": 1}
    st_base = np.cumsum([0] + [S_PE[k] for k, _, _ in BLOCKS]).tolist()
    sd_base = np.cumsum([0] + [S_DV[k] for k, _, _ in BLOCKS]).tolist()
    sa_base = np.cumsum([0] + [S_AC[k] for k, _, _ in BLOCKS]).tolist()
    sc_base = np.cumsum([0] + [NSC[k] for k, _, _ in BLOCKS]).tolist()
    nF_upto = np.cumsum([1 if k == "F" else 0 for k, _, _ in BLOCKS]).tolist()
    nT_upto = np.cumsum([1 if k == "T" else 0 for k, _, _ in BLOCKS]).tolist()

    def w1_stage(kind, fb):
        return (4 if kind == "F" else 1) + 1 + fb

    def w2_stage(kind, fb):
        return (12 if kind == "F" else 9) + 1 + fb

    NJ = {"F": 4, "T": 1}
    TW = {"F": 512, "T": 128}
    TSLOT = ["a", "b", "o0", "o1", "o2", "a", "b", "o0"]

    def tslot(base_sb, fb):
        sl = TSLOT[fb]
        if sl == "a":
            return base_sb, 0
        if sl == "b":
            return base_sb, 4096
        return sb_hT, 4096 + int(sl[1]) * 4096

    eo_map = [(0, 0), (1, 0), (1, 512), (2, 0), (2, 512), (3, 0), (3, 512), (4, 0)]

    def eo_tile(dt_):
        ten = (ps_a, ps_b, ps_b, ps_c, ps_c, ps_d, ps_d, ps_x)[dt_]
        col = (0, 0, 512, 0, 512, 0, 512, 0)[dt_]
        return ten, col

    bufd3 = bufd[:, :].rearrange("(n p) w -> n p w", p=128)

    with nc.Block() as blk:
        @blk.sync
        def _(s):
            s.wait_ge(ssc, 16 * NCH)
            s.wait_ge(sed, SB4 + 7)
            s.wait_ge(set_, TB4 + 5)
            rgf = s.alloc_register("p5s_nf")
            rgt_ = s.alloc_register("p5s_nt")
            for b, (kind, e, ii) in enumerate(BLOCKS):
                if kind == "F" and ii == 0:
                    if e > 0:
                        s.wait_ge(sbe, 32 * e)
                        s.wait_ge(sa5, sa_base[8 * e])
                    s.dma_start(out=sb_b1e[:, :], in_=b1T[e, :, :]).then_inc(sbe, 16)
                    s.dma_start(out=sb_b2e[:, :], in_=b2T[e, :, :]).then_inc(sbe, 16)
                    s.reg_load(rgf, sb_cct[0:1, e:e + 1])
                    s.reg_load(rgt_, sb_ctt[0:1, e:e + 1])
                rgx = rgf if kind == "F" else rgt_
                guards = s.If_cmp(rgx, ii, "IS_GT")
                guards.__enter__()
                if kind == "F":
                    if b > 0:
                        s.wait_ge(sd5, sd_base[b])
                        s.wait_ge(sbf, 16 * (nF_upto[b] - 1))
                    s.dma_start(out=sb_bfin[:, :].rearrange("p (j w) -> p j w", j=4),
                                in_=bufd[e * REG + ii * 512:e * REG + (ii + 1) * 512, :]
                                .rearrange("(j p) w -> p j w", p=128)).then_inc(sbf, 16)
                if kind == "F":
                    for f in range(8):
                        gidx = 8 * b + f
                        if gidx >= 2:
                            gp_ = gidx - 2
                            bp, fp = gp_ // 8, gp_ % 8
                            s.wait_ge(st5, st_base[bp] + w1_stage(BLOCKS[bp][0], fp))
                            s.wait_ge(sw1a if f % 2 == 0 else sw1b, 16 * (gp_ // 2 + 1))
                        s.dma_start(
                            out=sb_w1b[:, (f % 2) * 4096:(f % 2 + 1) * 4096],
                            in_=w1[e, f, :, :],
                        ).then_inc(sw1a if f % 2 == 0 else sw1b, 16)
                else:
                    kp = BLOCKS[b - 1][0]
                    T5b = st_base[b]
                    TP5 = st_base[b - 1]
                    if kp == "F":
                        og = [TP5 + w2_stage(kp, 3), TP5 + w2_stage(kp, 5),
                              TP5 + w2_stage(kp, 7)]
                    else:
                        og = [TP5 + w2_stage(kp, 7), TP5 + w2_stage(kp, 3),
                              TP5 + w2_stage(kp, 4)]
                    w1g = [None, None, og[0], og[1],
                           og[2], T5b + 2, T5b + 3, T5b + 4]
                    for f in range(8):
                        if w1g[f] is None:
                            gp_ = 8 * b + f - 2
                            bp, fp = gp_ // 8, gp_ % 8
                            s.wait_ge(st5, st_base[bp] + w1_stage(BLOCKS[bp][0], fp))
                        else:
                            s.wait_ge(st5, w1g[f])
                        s.wait_ge(stl1p[f % 2], 64 * (nT_upto[b] - 1) + 16 * (f // 2))
                        wb_, wo_ = tslot(sb_w1b, f)
                        s.dma_start(out=wb_[:, wo_:wo_ + 4096],
                                    in_=w1[e, f, :, :]).then_inc(stl1p[f % 2], 16)
                    for sem_, slot_ in [(sw1a, 2), (sw1b, 3)]:
                        s.wait_ge(sem_, 64 * b)
                        s.dma_start(out=junkd[b, slot_:slot_ + 1, 0:16],
                                    in_=sb_zero[0:1, 0:16]).then_inc(sem_, 64)
                guards.__exit__(None, None, None)
                with s.Else():
                    if kind == "F":
                        s.wait_ge(sbf, 16 * (nF_upto[b] - 1))
                        s.dma_start(out=junkd[b, 1:2, 0:16], in_=sb_zero[0:1, 0:16]
                                    ).then_inc(sbf, 16)
                    else:
                        for qi_, sem_ in enumerate(stl1p):
                            s.wait_ge(sem_, 64 * (nT_upto[b] - 1))
                            s.dma_start(out=junkd[b, 6 + qi_:7 + qi_, 0:16],
                                        in_=sb_zero[0:1, 0:16]).then_inc(sem_, 64)
                    s.wait_ge(sw1a, 64 * b)
                    s.dma_start(out=junkd[b, 2:3, 0:16], in_=sb_zero[0:1, 0:16]
                                ).then_inc(sw1a, 64)
                    s.wait_ge(sw1b, 64 * b)
                    s.dma_start(out=junkd[b, 3:4, 0:16], in_=sb_zero[0:1, 0:16]
                                ).then_inc(sw1b, 64)


        @blk.tensor
        def _(t):
            t.wait_ge(sed, SB4 + 7)
            rgf = t.alloc_register("p5t_nf")
            rgt2 = t.alloc_register("p5t_nt")
            for b, (kind, e, ii) in enumerate(BLOCKS):
                T5, D5, A5 = st_base[b], sd_base[b], sa_base[b]
                nj, tw = NJ[kind], TW[kind]
                if kind == "F" and ii == 0:
                    t.reg_load(rgf, sb_cct[0:1, e:e + 1])
                    t.reg_load(rgt2, sb_ctt[0:1, e:e + 1])
                guard = t.If_cmp(rgf if kind == "F" else rgt2, ii, "IS_GT")
                guard.__enter__()
                t.wait_ge(sbf if kind == "F" else sbf2,
                          16 * (nF_upto[b] if kind == "F" else nT_upto[b]))
                if b > 0:
                    t.wait_ge(sd5, D5)
                for j in range(nj):
                    if j > 0:
                        t.wait_ge(sd5, D5 + j)
                    for it in range(8):
                        ins = t.transpose(ps_d[:, it * 128:(it + 1) * 128],
                                          sb_bfin[:, j * RW + it * 128:j * RW + (it + 1) * 128],
                                          sb_const[:, :])
                    ins.then_inc(st5, 1)
                t.wait_ge(sd5, D5 + nj)
                for fb in range(8):
                    gidx = 8 * b + fb
                    if kind == "T":
                        t.wait_ge(stl1p[fb % 2], 64 * (nT_upto[b] - 1) + 16 * (fb // 2 + 1))
                        w1sb, wbuf = tslot(sb_w1b, fb)
                    else:
                        t.wait_ge(sw1a if fb % 2 == 0 else sw1b, 16 * (gidx // 2 + 1))
                        w1sb, wbuf = sb_w1b, (fb % 2) * 4096
                    if fb >= 2:
                        t.wait_ge(sa5, A5 + fb - 1)
                    if fb % 2 == 0:
                        hps = [(ps_b, 0), (ps_b, 512), (ps_c, 0), (ps_c, 512)]
                    else:
                        hps = [(ps_d, 0), (ps_d, 512), (ps_a, 0), (ps_x, 0)]
                    for it in range(8):
                        for fs in range(4):
                            ten, col = hps[fs]
                            ins = t.matmul(
                                ten[:, col:col + tw],
                                w1sb[:, wbuf + it * 512 + fs * 128:wbuf + it * 512 + (fs + 1) * 128],
                                sb_xtf[:, it * tw:(it + 1) * tw],
                                start=(it == 0), stop=(it == 7))
                    ins.then_inc(st5, 1)
                t.wait_ge(sa5, A5 + 8)
                for fb2 in range(8):
                    gidx = 8 * b + fb2
                    if kind == "T":
                        t.wait_ge(stl2p[fb2 % 2], 64 * (nT_upto[b] - 1) + 16 * (fb2 // 2 + 1))
                        w2sb, wbuf = tslot(sb_w2b, fb2)
                    else:
                        t.wait_ge(sw2a if fb2 % 2 == 0 else sw2b, 16 * (gidx // 2 + 1))
                        w2sb, wbuf = sb_w2b, (fb2 % 2) * 4096
                    for fs2 in range(4):
                        for dt_ in range(8):
                            ten, col = eo_tile(dt_)
                            ins = t.matmul(
                                ten[:, col:col + tw],
                                w2sb[:, wbuf + fs2 * 1024 + dt_ * 128:wbuf + fs2 * 1024 + (dt_ + 1) * 128],
                                sb_hT[:, (fb2 * 4 + fs2) * tw:(fb2 * 4 + fs2 + 1) * tw],
                                start=(fb2 == 0 and fs2 == 0),
                                stop=(fb2 == 7 and fs2 == 3))
                    ins.then_inc(st5, 1)
                for j in range(nj):
                    t.wait_ge(sd5, D5 + nj + 1 + 3 * j)
                    for dt_ in range(8):
                        ins = t.transpose(ps_d[:, dt_ * 128:(dt_ + 1) * 128],
                                          sb_eoT[:, dt_ * tw + j * 128:dt_ * tw + (j + 1) * 128],
                                          sb_const[:, :])
                    ins.then_inc(st5, 1)
                guard.__exit__(None, None, None)
                with t.Else():
                    t.wait_ge(sd5, D5 + nj + 1)
                    t.matmul(ps_x[0:1, 300:301], sb_ones_r[0:1, 0:1],
                             sb_ones_r[0:1, 0:1], start=True, stop=True
                             ).then_inc(st5, S_PE[kind])

        @blk.scalar
        def _(sc):
            sc = AutoDrain(sc)
            sc.wait_ge(sed, SB4 + 7)
            rgf = sc.alloc_register("p5a_nf")
            rgt2 = sc.alloc_register("p5a_nt")
            for b, (kind, e, ii) in enumerate(BLOCKS):
                T5, D5, A5 = st_base[b], sd_base[b], sa_base[b]
                nj, tw = NJ[kind], TW[kind]
                if kind == "F" and ii == 0:
                    sc.wait_ge(sbe, 32 * (e + 1))
                    sc.reg_load(rgf, sb_cct[0:1, e:e + 1])
                    sc.reg_load(rgt2, sb_ctt[0:1, e:e + 1])
                guard = sc.If_cmp(rgf if kind == "F" else rgt2, ii, "IS_GT")
                guard.__enter__()
                if kind == "T":
                    w2g = [None, None, T5 + 9, T5 + 5, T5 + 6,
                           T5 + 10, T5 + 11, T5 + 12]
                for f in range(2):
                    if 8 * b + f >= 2:
                        gp_ = 8 * b + f - 2
                        bp, fp = gp_ // 8, gp_ % 8
                        sc.wait_ge(st5, st_base[bp] + w2_stage(BLOCKS[bp][0], fp))
                    if kind == "T":
                        sc.wait_ge(stl2p[f % 2], 64 * (nT_upto[b] - 1) + 16 * (f // 2))
                        wb_, wo_ = tslot(sb_w2b, f)
                        sc.dma_start(out=wb_[:, wo_:wo_ + 4096],
                                     in_=w2[e, f, :, :]).then_inc(stl2p[f % 2], 16)
                    else:
                        if 8 * b + f >= 2:
                            sc.wait_ge(sw2a if f % 2 == 0 else sw2b,
                                       16 * ((8 * b + f - 2) // 2 + 1))
                        sc.dma_start(
                            out=sb_w2b[:, (f % 2) * 4096:(f % 2 + 1) * 4096],
                            in_=w2[e, f, :, :],
                        ).then_inc(sw2a if f % 2 == 0 else sw2b, 16)
                for fb in range(8):
                    sc.wait_ge(st5, T5 + w1_stage(kind, fb))
                    if fb % 2 == 0:
                        hps = [(ps_b, 0), (ps_b, 512), (ps_c, 0), (ps_c, 512)]
                    else:
                        hps = [(ps_d, 0), (ps_d, 512), (ps_a, 0), (ps_x, 0)]
                    for fs in range(4):
                        ten, col = hps[fs]
                        sc.activation(out=sb_hT[:, (fb * 4 + fs) * tw:(fb * 4 + fs + 1) * tw],
                                      in_=ten[:, col:col + tw], func=AF.Gelu,
                                      bias=sb_b1e[:, fb * 4 + fs:fb * 4 + fs + 1])
                    sc.drain().then_inc(sa5, 1)
                for f in range(2, 8):
                    if kind == "T":
                        sc.wait_ge(st5, w2g[f])
                        sc.wait_ge(stl2p[f % 2], 64 * (nT_upto[b] - 1) + 16 * (f // 2))
                        wb_, wo_ = tslot(sb_w2b, f)
                        sc.dma_start(out=wb_[:, wo_:wo_ + 4096],
                                     in_=w2[e, f, :, :]).then_inc(stl2p[f % 2], 16)
                    else:
                        gp_ = 8 * b + f - 2
                        bp, fp = gp_ // 8, gp_ % 8
                        sc.wait_ge(st5, st_base[bp] + w2_stage(BLOCKS[bp][0], fp))
                        sc.wait_ge(sw2a if f % 2 == 0 else sw2b, 16 * (gp_ // 2 + 1))
                        sc.dma_start(
                            out=sb_w2b[:, (f % 2) * 4096:(f % 2 + 1) * 4096],
                            in_=w2[e, f, :, :],
                        ).then_inc(sw2a if f % 2 == 0 else sw2b, 16)
                if kind == "T":
                    for sem_, slot_ in [(sw2a, 4), (sw2b, 5)]:
                        sc.wait_ge(sem_, 64 * b)
                        sc.dma_start(out=junkd[b, slot_:slot_ + 1, 0:16],
                                     in_=sb_zero[0:1, 0:16]).then_inc(sem_, 64)
                for j in range(nj):
                    sc.wait_ge(sd5, D5 + nj + 1 + 3 * j + 1)
                    sc.activation(out=sb_sq2[:, :], in_=sb_wb[:, :], func=AF.Square,
                                  accum_out=sb_r5[:, 1:2])
                    sc.drain().then_inc(sa5, 1)
                    sc.wait_ge(sd5, D5 + nj + 1 + 3 * j + 2)
                    sc.activation(out=sb_r5[:, 4:5], in_=sb_r5[:, 3:4], func=AF.Sqrt,
                                  bias=sb_eps[:, :])
                    sc.drain().then_inc(sa5, 1)
                guard.__exit__(None, None, None)
                with sc.Else():
                    sc.wait_ge(sw2a, 64 * b)
                    sc.dma_start(out=junkd[b, 4:5, 0:16], in_=sb_zero[0:1, 0:16]
                                 ).then_inc(sw2a, 64)
                    sc.wait_ge(sw2b, 64 * b)
                    sc.dma_start(out=junkd[b, 5:6, 0:16], in_=sb_zero[0:1, 0:16]
                                 ).then_inc(sw2b, 64)
                    if kind == "T":
                        for qi_, sem_ in enumerate(stl2p):
                            sc.wait_ge(sem_, 64 * (nT_upto[b] - 1))
                            sc.dma_start(out=junkd[b, 8 + qi_:9 + qi_, 0:16],
                                         in_=sb_zero[0:1, 0:16]).then_inc(sem_, 64)
                    sc.drain().then_inc(sa5, S_AC[kind])

        @blk.vector
        def _(v):
            v = AutoDrain(v)
            v.wait_ge(sed, SB4 + 7)
            rgf = v.alloc_register("p5v_nf")
            rgt2 = v.alloc_register("p5v_nt")
            for b, (kind, e, ii) in enumerate(BLOCKS):
                T5, D5, A5 = st_base[b], sd_base[b], sa_base[b]
                nj, tw = NJ[kind], TW[kind]
                if kind == "F" and ii == 0:
                    v.reg_load(rgf, sb_cct[0:1, e:e + 1])
                    v.reg_load(rgt2, sb_ctt[0:1, e:e + 1])
                guardv = v.If_cmp(rgf if kind == "F" else rgt2, ii, "IS_GT")
                guardv.__enter__()
                for j in range(nj):
                    v.wait_ge(st5, T5 + j + 1)
                    for it in range(8):
                        v.tensor_copy(out=sb_xtf[:, it * tw + j * 128:it * tw + (j + 1) * 128],
                                      in_=ps_d[:, it * 128:(it + 1) * 128])
                    v.drain().then_inc(sd5, 1)
                v.wait_ge(st5, T5 + w2_stage(kind, 7))
                for dt_ in range(8):
                    ten, col = eo_tile(dt_)
                    v.tensor_scalar(out=sb_eoT[:, dt_ * tw:(dt_ + 1) * tw],
                                    in0=ten[:, col:col + tw],
                                    scalar1=sb_b2e[:, dt_:dt_ + 1], scalar2=None,
                                    op0=ALU.add)
                v.drain().then_inc(sd5, 1)
                for j in range(nj):
                    v.wait_ge(st5, T5 + S_PE[kind] - nj + j + 1)
                    if sc_base[b] + j > 0:
                        v.wait_ge(swb, 16 * (sc_base[b] + j))
                    if kind == "F":
                        v.tensor_scalar_add(sb_r5[:, 6:7], sb_iota_c[:, :],
                                            float(ii * 512 + j * 128))
                    else:
                        v.tensor_tensor(out=sb_r5[:, 6:7], in0=sb_iota_c[:, :],
                                        in1=sb_sbase[:, (e * 4 + ii):(e * 4 + ii) + 1],
                                        op=ALU.add)
                    v.tensor_tensor(out=sb_r5[:, 7:8], in0=sb_r5[:, 6:7],
                                    in1=sb_thrbc[:, e:e + 1], op=ALU.is_lt)
                    v.tensor_tensor(out=sb_r5[:, 8:9], in0=sb_r5[:, 7:8],
                                    in1=sb_bfin[:, j * RW + 1024:j * RW + 1025],
                                    op=ALU.mult)
                    v.scalar_tensor_tensor(out=sb_wb[:, :], in0=ps_d[:, :],
                                           scalar=sb_r5[:, 8:9],
                                           in1=sb_bfin[:, j * RW:j * RW + 1024],
                                           op0=ALU.mult, op1=ALU.add)
                    v.reduce_sum(sb_r5[:, 0:1], sb_wb[:, :], axis=AX.X)
                    v.tensor_scalar_mul(sb_r5[:, 0:1], sb_r5[:, 0:1], 1.0 / 1024)
                    v.drain().then_inc(sd5, 1)
                    v.wait_ge(sa5, A5 + 8 + 2 * j + 1)
                    v.tensor_tensor(out=sb_r5[:, 2:3], in0=sb_r5[:, 0:1],
                                    in1=sb_r5[:, 0:1], op=ALU.mult)
                    v.tensor_scalar_mul(sb_r5[:, 3:4], sb_r5[:, 1:2], 1.0 / 1024)
                    v.tensor_tensor(out=sb_r5[:, 3:4], in0=sb_r5[:, 3:4],
                                    in1=sb_r5[:, 2:3], op=ALU.subtract)
                    v.drain().then_inc(sd5, 1)
                    v.wait_ge(sa5, A5 + 8 + 2 * j + 2)
                    v.reciprocal(sb_r5[:, 5:6], sb_r5[:, 4:5])
                    v.tensor_scalar(out=sb_wb2[:, :], in0=sb_wb[:, :],
                                    scalar1=sb_r5[:, 0:1], scalar2=sb_r5[:, 5:6],
                                    op0=ALU.subtract, op1=ALU.mult)
                    v.tensor_tensor(out=sb_r5[:, 9:10], in0=sb_r5[:, 6:7],
                                    in1=sb_cntbc[:, e:e + 1], op=ALU.is_lt)
                    v.tensor_tensor(out=sb_r5[:, 10:11],
                                    in0=sb_bfin[:, j * RW + 1025:j * RW + 1026],
                                    in1=sb_r5[:, 9:10], op=ALU.mult)
                    v.tensor_scalar(out=sb_r5[:, 11:12], in0=sb_r5[:, 9:10],
                                    scalar1=-float(T), scalar2=float(T),
                                    op0=ALU.mult, op1=ALU.add)
                    v.tensor_tensor(out=sb_r5[:, 10:11], in0=sb_r5[:, 10:11],
                                    in1=sb_r5[:, 11:12], op=ALU.add)
                    v.tensor_copy(out=sb_dsw[:, :], in_=sb_r5[:, 10:11])
                    v.drain().then_inc(sd5, 1)
                guardv.__exit__(None, None, None)
                with v.Else():
                    v.drain().then_inc(sd5, S_DV[kind])

        @blk.gpsimd
        def _(g):
            g.wait_ge(sed, SB4 + 7)
            rgf = g.alloc_register("p5g_nf")
            rgt2 = g.alloc_register("p5g_nt")
            for b, (kind, e, ii) in enumerate(BLOCKS):
                D5 = sd_base[b]
                nj = NJ[kind]
                if kind == "F" and ii == 0:
                    g.reg_load(rgf, sb_cct[0:1, e:e + 1])
                    g.reg_load(rgt2, sb_ctt[0:1, e:e + 1])
                with g.If_cmp(rgf if kind == "F" else rgt2, ii, "IS_GT"):
                    if kind == "T":
                        if b > 0:
                            g.wait_ge(sd5, D5)
                            g.wait_ge(sbf2, 16 * (nT_upto[b] - 1))
                        g.indirect_dma_start(
                            out=sb_bfin[:, 0:RW], out_offset=None,
                            in_=bufd[:, :],
                            in_offset=IndirectOffsetOnAxis(
                                sb_tidx[:, (e * 4 + ii):(e * 4 + ii) + 1], 0),
                        ).then_inc(sbf2, 16)
                    for j in range(nj):
                        g.wait_ge(sd5, D5 + nj + 1 + 3 * j + 3)
                        g.indirect_dma_start(
                            out=yb[:, :],
                            out_offset=IndirectOffsetOnAxis(sb_dsw[:, :], 0),
                            in_=sb_wb2[:, :], in_offset=None,
                        ).then_inc(swb, 16)
                with g.Else():
                    if kind == "T":
                        g.wait_ge(sbf2, 16 * (nT_upto[b] - 1))
                        g.dma_start(out=junkd[b, 1:2, 0:16], in_=sb_zero[0:1, 0:16]
                                    ).then_inc(sbf2, 16)
                    g.wait_ge(swb, 16 * sc_base[b])
                    g.dma_start(out=junkd[b, 0:1, 0:16], in_=sb_zero[0:1, 0:16]
                                ).then_inc(swb, 16 * NSC[kind])
            pass

    with nc.Block() as blkf:
        @blkf.sync
        def _(s):
            s.wait_ge(swb, 16 * sc_base[NBLK])
            s.dma_start(out=y[0:768, :].rearrange("(n p) c -> p n c", p=128),
                        in_=yb[0:768, :].rearrange("(n p) c -> p n c", p=128)
                        ).then_inc(sfin, 16)

        @blkf.scalar
        def _(sc):
            sc.wait_ge(swb, 16 * sc_base[NBLK])
            sc.dma_start(out=y[768:1536, :].rearrange("(n p) c -> p n c", p=128),
                         in_=yb[768:1536, :].rearrange("(n p) c -> p n c", p=128)
                         ).then_inc(sfin, 16)

        @blkf.gpsimd
        def _(g):
            g.wait_ge(swb, 16 * sc_base[NBLK])
            g.dma_start(out=y[1536:2048, :].rearrange("(n p) c -> p n c", p=128),
                        in_=yb[1536:2048, :].rearrange("(n p) c -> p n c", p=128)
                        ).then_inc(sfin2, 16)
            g.wait_ge(sfin, 32)
            g.wait_ge(sfin2, 16)

    return nc


_NC_CACHE = {}


def _get_nc():
    if "nc" not in _NC_CACHE:
        _NC_CACHE["nc"] = build_nc(debug=0)
    return _NC_CACHE["nc"]


def _stage(inp):
    x0 = np.ascontiguousarray(
        np.asarray(inp["L0_attn_hidden_states"], np.float32).reshape(16384, 1024))
    Wsm = np.concatenate(
        [np.asarray(inp["L0_attn_Wb"], np.float32),
         np.asarray(inp["L0_attn_Wfd"], np.float32),
         np.asarray(inp["L0_attn_Wsd"], np.float32),
         np.asarray(inp["L0_attn_Wfg"], np.float32),
         np.asarray(inp["L0_attn_Wsg"], np.float32),
         np.asarray(inp["L0_ffn_router_w"], np.float32).T], axis=1).astype(np.float32)
    wsmb = np.zeros((1, 88), np.float32)
    wsmb[0, 16:32] = np.asarray(inp["L0_attn_fd_bias"], np.float32)
    wsmb[0, 32:48] = np.asarray(inp["L0_attn_sd_bias"], np.float32)
    common = {
        "wq": np.ascontiguousarray(np.asarray(inp["L0_attn_Wq"], np.float32)),
        "wk": np.ascontiguousarray(np.asarray(inp["L0_attn_Wk"], np.float32)),
        "wv": np.ascontiguousarray(np.asarray(inp["L0_attn_Wv"], np.float32)),
        "wsm": np.ascontiguousarray(Wsm),
        "wsmb": wsmb,
        "wo": np.ascontiguousarray(np.asarray(inp["L0_attn_Wo"], np.float32)),
        "woT": np.ascontiguousarray(np.asarray(inp["L0_attn_Wo"], np.float32).T),
        "rT": np.ascontiguousarray(np.asarray(inp["L0_ffn_router_w"], np.float32).T),
        "w1": np.ascontiguousarray(
            np.asarray(inp["L0_ffn_w1"], np.float32)
            .reshape(8, 8, 128, 8, 512).transpose(0, 3, 2, 1, 4)
            .reshape(8, 8, 128, 4096)),
        "w2": np.ascontiguousarray(
            np.asarray(inp["L0_ffn_w2"], np.float32)
            .reshape(8, 8, 4, 128, 1024).transpose(0, 1, 3, 2, 4)
            .reshape(8, 8, 128, 4096)),
        "b1T": np.ascontiguousarray(
            np.asarray(inp["L0_ffn_b1"], np.float32).reshape(8, 32, 128).transpose(0, 2, 1)),
        "b2T": np.ascontiguousarray(
            np.asarray(inp["L0_ffn_b2"], np.float32).reshape(8, 8, 128).transpose(0, 2, 1)),
    }
    in_maps = []
    for c in range(NCORE):
        sl = x0[T * c: T * (c + 1)]
        xTc = np.ascontiguousarray(sl.reshape(NCH, 128, 1024).transpose(0, 2, 1))
        segp = np.zeros((32, 8), np.float32)
        for j in range(NCORE):
            if j < c and j // 4 == c // 4:
                segp[:, j] = 1.0
        cntp = np.zeros((8, 1), np.float32)
        cntp[:c, 0] = 1.0
        m = dict(common)
        m.update({"xT": xTc, "x0r": np.ascontiguousarray(sl),
                  "segp": segp, "cntp": cntp})
        in_maps.append(m)
    return in_maps


def kernel(**inputs):
    from concourse.bass_utils import run_bass_kernel_spmd
    in_maps = _stage(inputs)
    nc = _get_nc()
    res = run_bass_kernel_spmd(nc, in_maps, core_ids=list(range(NCORE)))
    out = np.concatenate([res.results[c]["y"] for c in range(NCORE)], axis=0)
    return out.reshape(2, 8192, 1024)


# version 17
# speedup vs baseline: 1.0430x; 1.0430x over previous
import numpy as np
import concourse.bass as bass
import concourse.mybir as mybir
from concourse.bass import IndirectOffsetOnAxis
from concourse.ap import AP

F32 = mybir.dt.float32
I32 = mybir.dt.int32
U8 = mybir.dt.uint8
AF = mybir.ActivationFunctionType
ALU = mybir.AluOpType
AX = mybir.AxisListType

H, HD, D, CHUNK, NCH = 16, 64, 1024, 128, 16
T = CHUNK * NCH
E, FF, CAP = 8, 4096, 3072
NCORE = 8
REG = 2048
RW = 1032

TPC, APC, DPC, SSPC = 11, 7, 12, 48


class AutoDrain:
    _OPS = {"tensor_tensor", "tensor_scalar", "tensor_scalar_add",
            "tensor_scalar_mul", "tensor_copy", "tensor_reduce", "reduce_sum",
            "reduce_max", "reciprocal", "activation", "scalar_tensor_tensor",
            "memset", "iota", "select", "copy_predicated", "max", "max_index",
            "tensor_single_scalar", "tensor_add", "tensor_sub", "tensor_mul"}
    _OUT_KW = ("out", "ap")

    def __init__(self, eng):
        self._e = eng
        self._written = []
        self._read = []

    @staticmethod
    def _foot(ap):
        try:
            name = ap.tensor.name
            pdim = ap.ap[0]
            p0 = ap.base_partition() if callable(getattr(ap, "base_partition", None)) else 0
            np_ = pdim[1]
            steps = [(s, c) for s, c in ap.ap[1:]]
            span = 0
            for s, c in steps:
                span += abs(s) * (c - 1)
            item = 4
            off = ap.offset % ap.ap[0][0] if ap.ap[0][0] else ap.offset
            return (name, p0, p0 + np_, off * item, (off + span + 1) * item)
        except Exception:
            return (getattr(getattr(ap, "tensor", None), "name", "?"),
                    0, 128, 0, 1 << 40)

    @classmethod
    def _olap(cls, a, b):
        return (a[0] == b[0] and a[1] < b[2] and b[1] < a[2]
                and a[3] < b[4] and b[3] < a[4])

    def _aps_of(self, args, kwargs):
        from concourse.ap import AP as _AP
        ins, outs = [], []
        for k, v in kwargs.items():
            if isinstance(v, _AP):
                if k in ("out", "accum_out", "out_ap", "ap"):
                    outs.append(v)
                else:
                    ins.append(v)
        first = True
        for a in args:
            if isinstance(a, _AP):
                if first and not any(k in kwargs for k in ("out", "out_ap", "ap")):
                    outs.append(a)
                else:
                    ins.append(a)
                first = False
        return ins, outs

    def __getattr__(self, name):
        attr = getattr(self._e, name)
        if name == "drain":
            def wrapped_drain(*a, **k):
                self._written = []
                self._read = []
                return attr(*a, **k)
            return wrapped_drain
        if name == "wait_ge":
            return attr
        if name in self._OPS:
            def wrapped(*a, **k):
                ins, outs = self._aps_of(a, k)
                fin = [self._foot(x) for x in ins]
                fout = [self._foot(x) for x in outs]
                need = (any(self._olap(w, f) for w in self._written
                            for f in fin + fout)
                        or any(self._olap(r, f) for r in self._read for f in fout))
                if need:
                    self._e.drain()
                    self._written = []
                    self._read = []
                ins2 = attr(*a, **k)
                self._written.extend(fout)
                self._read.extend(fin)
                return ins2
            return wrapped
        return attr


def build_nc(debug=0):
    nc = bass.Bass("TRN2", target_bir_lowering=False, debug=False)

    xT = nc.dram_tensor("xT", [NCH, D, CHUNK], F32, kind="ExternalInput").ap()
    x0r = nc.dram_tensor("x0r", [T, D], F32, kind="ExternalInput").ap()
    wq = nc.dram_tensor("wq", [D, D], F32, kind="ExternalInput").ap()
    wk = nc.dram_tensor("wk", [D, D], F32, kind="ExternalInput").ap()
    wv = nc.dram_tensor("wv", [D, D], F32, kind="ExternalInput").ap()
    wsm = nc.dram_tensor("wsm", [D, 88], F32, kind="ExternalInput").ap()
    wsmb = nc.dram_tensor("wsmb", [1, 88], F32, kind="ExternalInput").ap()
    wo = nc.dram_tensor("wo", [D, D], F32, kind="ExternalInput").ap()
    woT = nc.dram_tensor("woT", [D, D], F32, kind="ExternalInput").ap()
    rT = nc.dram_tensor("rT", [D, E], F32, kind="ExternalInput").ap()
    w1 = nc.dram_tensor("w1", [E, 8, 128, 4096], F32, kind="ExternalInput").ap()
    w2 = nc.dram_tensor("w2", [E, 8, 128, 4096], F32, kind="ExternalInput").ap()
    b1T = nc.dram_tensor("b1T", [E, 128, 32], F32, kind="ExternalInput").ap()
    b2T = nc.dram_tensor("b2T", [E, 128, 8], F32, kind="ExternalInput").ap()
    segp = nc.dram_tensor("segp", [32, 8], F32, kind="ExternalInput").ap()
    cntp = nc.dram_tensor("cntp", [8, 1], F32, kind="ExternalInput").ap()

    qfsp = nc.dram_tensor("qfsp", [NCH, D, CHUNK], F32,
                          kind="ExternalOutput" if debug else "Internal").ap()
    qssp = nc.dram_tensor("qssp", [NCH, D, CHUNK], F32,
                          kind="ExternalOutput" if debug else "Internal").ap()
    otsp = nc.dram_tensor("otsp", [NCH, D, CHUNK], F32,
                          kind="ExternalOutput" if debug else "Internal").ap()
    bufd = nc.dram_tensor("bufd", [E * REG, RW], F32).ap()
    cc1i = nc.dram_tensor("cc1i", [32, 4097], F32).ap()
    cc1o = nc.dram_tensor("cc1o", [NCORE * 32, 4097], F32).ap()
    cc2i = nc.dram_tensor("cc2i", [1, 8], F32).ap()
    cc2o = nc.dram_tensor("cc2o", [NCORE, 8], F32).ap()
    sentb = nc.dram_tensor("sentb", [32, 4096], F32).ap()
    junkd = nc.dram_tensor("junkd", [64, 10, 16], F32).ap()

    y = nc.dram_tensor("y", [T, D], F32, kind="ExternalOutput").ap()
    yb = nc.dram_tensor("yb", [T + 128, D], F32).ap()
    if debug:
        dbg_state = nc.dram_tensor("dbg_state", [128, 1024], F32, kind="ExternalOutput").ap()
        dbg_gall = nc.dram_tensor("dbg_gall", [32, NCH], F32, kind="ExternalOutput").ap()
        dbg_gloc = nc.dram_tensor("dbg_gloc", [32, NCH + 1], F32, kind="ExternalOutput").ap()
        dbg_lx0 = nc.dram_tensor("dbg_lx0", [128, NCH * 8], F32, kind="ExternalOutput").ap()
        dbg_cc1o = nc.dram_tensor("dbg_cc1o", [NCORE * 32, 4097], F32, kind="ExternalOutput").ap()
        dbg_x1 = nc.dram_tensor("dbg_x1", [128, NCH * RW], F32, kind="ExternalOutput").ap()
        dbg_oh = nc.dram_tensor("dbg_oh", [128, NCH * 8], F32, kind="ExternalOutput").ap()
        dbg_dest = nc.dram_tensor("dbg_dest", [128, NCH], I32, kind="ExternalOutput").ap()
        dbg_rows = nc.dram_tensor("dbg_rows", [8, 64], F32, kind="ExternalOutput").ap()

    sb_const = nc.alloc_sbuf_tensor("c_ident", [128, 128], F32).ap()
    sb_u128 = nc.alloc_sbuf_tensor("c_u128", [128, 128], F32).ap()
    sb_ones_c = nc.alloc_sbuf_tensor("c_ones_c", [128, 1], F32).ap()
    sb_ones_r = nc.alloc_sbuf_tensor("c_ones_r", [1, 128], F32).ap()
    sb_iota_c = nc.alloc_sbuf_tensor("c_iota_c", [128, 1], F32).ap()
    sb_iota8 = nc.alloc_sbuf_tensor("c_iota8", [128, 8], F32).ap()
    sb_rowi = nc.alloc_sbuf_tensor("c_rowi", [128, 128], F32).ap()
    sb_coli = nc.alloc_sbuf_tensor("c_coli", [128, 128], F32).ap()
    sb_esel = nc.alloc_sbuf_tensor("c_esel", [16, 8 * 128], F32).ap()
    sb_m5 = nc.alloc_sbuf_tensor("c_m5", [128, 1], F32).ap()
    sb_eps = nc.alloc_sbuf_tensor("c_eps", [128, 1], F32).ap()

    sb_gloc = nc.alloc_sbuf_tensor("p1_gloc", [32, NCH + 1], F32).ap()
    sb_gall = nc.alloc_sbuf_tensor("p1_gall", [32, NCH], F32).ap()
    sb_lx0 = nc.alloc_sbuf_tensor("p1_lx0", [128, NCH * 8], F32).ap()
    sb_zero = nc.alloc_sbuf_tensor("c_zero", [128, RW], F32).ap()
    sb_o128 = nc.alloc_sbuf_tensor("c_o128", [128, 128], F32).ap()
    sb_adder = nc.alloc_sbuf_tensor("c_adder", [128, 8], F32).ap()
    sb_thrbc = nc.alloc_sbuf_tensor("p4_thrbc", [128, 8], F32).ap()
    sb_cntbc = nc.alloc_sbuf_tensor("p4_cntbc", [128, 8], F32).ap()
    sb_cct = nc.alloc_sbuf_tensor("p4_cct", [1, 8], I32).ap()
    sb_ctt = nc.alloc_sbuf_tensor("p4_ctt", [1, 8], I32).ap()
    sb_cti = nc.alloc_sbuf_tensor("p4_cti", [1, 8], I32).ap()
    sb_nff = nc.alloc_sbuf_tensor("p4_nff", [1, 8], F32).ap()
    sb_sbrow = nc.alloc_sbuf_tensor("p4_sbrow", [1, 64], F32).ap()
    sb_sbase = nc.alloc_sbuf_tensor("p4_sbase", [128, 64], F32).ap()
    sb_tidx = nc.alloc_sbuf_tensor("p4_tidx", [128, 32], I32).ap()
    sb_e2k = nc.alloc_sbuf_tensor("p4_e2k", [1, 32], F32).ap()
    sb_io4r = nc.alloc_sbuf_tensor("p4_io4r", [1, 32], F32).ap()
    sb_cntrow = nc.alloc_sbuf_tensor("p4_cntrow", [1, 8], F32).ap()

    sx0 = nc.alloc_semaphore("sx0")
    sx1 = nc.alloc_semaphore("sx1")
    sspq = nc.alloc_semaphore("sspq")
    sspo = nc.alloc_semaphore("sspo")
    scw = nc.alloc_semaphore("scw")
    szf = nc.alloc_semaphore("szf")
    sed = nc.alloc_semaphore("sed")
    set_ = nc.alloc_semaphore("set")
    sea = nc.alloc_semaphore("sea")
    sg2 = nc.alloc_semaphore("sg2")
    sco = nc.alloc_semaphore("sco")
    syc0 = nc.alloc_semaphore("syc0")
    syc1 = nc.alloc_semaphore("syc1")
    ssc = nc.alloc_semaphore("ssc")
    sp4 = nc.alloc_semaphore("sp4")
    ssj = nc.alloc_semaphore("ssj")
    ssb = nc.alloc_semaphore("ssb")
    s2i = nc.alloc_semaphore("s2i")
    s2o = nc.alloc_semaphore("s2o")
    tp = nc.alloc_semaphore("tp")
    ap_ = nc.alloc_semaphore("apx")
    dp = nc.alloc_semaphore("dpx")
    gp = nc.alloc_semaphore("gpx")
    ssp = nc.alloc_semaphore("ssp")
    ccs = nc.alloc_semaphore("ccs")
    sq = nc.alloc_semaphore("sqx")

    with nc.Block() as blk:
        @blk.gpsimd
        def _(g):
            g = AutoDrain(g)
            g.iota(sb_iota_c[:, :], pattern=[[0, 1]], base=0, channel_multiplier=1,
                   allow_small_or_imprecise_dtypes=True)
            g.iota(sb_iota8[:, :], pattern=[[1, 8]], base=0, channel_multiplier=0,
                   allow_small_or_imprecise_dtypes=True)
            g.iota(sb_rowi[:, :], pattern=[[0, 128]], base=0, channel_multiplier=1,
                   allow_small_or_imprecise_dtypes=True)
            g.iota(sb_coli[:, :], pattern=[[1, 128]], base=0, channel_multiplier=0,
                   allow_small_or_imprecise_dtypes=True)
            g.memset(sb_ones_c[:, :], 1.0)
            g.memset(sb_ones_r[:, :], 1.0)
            g.memset(sb_m5[:, :], -5.0)
            g.memset(sb_eps[:, :], 1e-5)
            g.memset(sb_zero[:, :], 0.0)
            g.memset(sb_o128[:, :], 1.0)
            g.memset(sb_gloc[:, :], 1.0)
            g.iota(sb_esel[:, :], pattern=[[2, 8], [1, 2], [0, 64]], base=0,
                   channel_multiplier=0, allow_small_or_imprecise_dtypes=True)
            g.iota(sb_io4r[:, :], pattern=[[0, 8], [128, 4]], base=0,
                   channel_multiplier=0, allow_small_or_imprecise_dtypes=True)
            g.iota(sb_e2k[:, :], pattern=[[2048, 8], [0, 4]], base=0,
                   channel_multiplier=0, allow_small_or_imprecise_dtypes=True)
            g.drain().then_inc(gp, 1)

        @blk.vector
        def _(v):
            v = AutoDrain(v)
            v.wait_ge(gp, 1)
            v.tensor_tensor(out=sb_const[:, :], in0=sb_rowi[:, :], in1=sb_coli[:, :],
                            op=ALU.is_equal)
            v.tensor_tensor(out=sb_u128[:, :], in0=sb_rowi[:, :], in1=sb_coli[:, :],
                            op=ALU.is_le)
            v.tensor_tensor(out=sb_esel[:, :], in0=sb_esel[:, :],
                            in1=sb_rowi[0:16, 0:1].broadcast_to([16, 1024]),
                            op=ALU.is_equal)
            v.tensor_scalar(out=sb_adder[:, :], in0=sb_iota8[:, :],
                            scalar1=float(REG), scalar2=-1.0, op0=ALU.mult, op1=ALU.add)
            v.drain().then_inc(dp, 2)

    _p1_guards = []

    def _scoped(name, shape, dtype=F32):
        g_ = nc.sbuf_tensor(name, shape, dtype)
        h_ = g_.__enter__()
        _p1_guards.append(g_)
        return h_.ap()

    sb_state = _scoped("p1_state", [128, 1024])
    sb_wq = _scoped("w_q", [128, 8 * 1024])
    sb_wk = _scoped("w_k", [128, 8 * 1024])
    sb_wv = _scoped("w_v", [128, 8 * 1024])
    sb_wsm = _scoped("w_sm", [128, 8 * 88])
    sb_wsmb = _scoped("w_smb", [1, 88])

    sb_xt = _scoped("p1_xt", [128, 2 * 1024])
    sb_ksb = _scoped("p1_ksb", [128, 1024])
    sb_vsb = _scoped("p1_vsb", [128, 1024])
    sb_vs2 = _scoped("p1_vs2", [128, 1024])
    sb_qsb = _scoped("p1_qsb", [128, 1024])
    sb_qaf = _scoped("p1_qaf", [128, 1024])
    sb_qas = _scoped("p1_qas", [128, 1024])
    sb_ot = _scoped("p1_ot", [128, 1024])
    sb_sig = _scoped("p1_sig", [128, 80])
    sb_stat = _scoped("p1_stat", [128, 96])
    sb_afas = _scoped("p1_afas", [128, 32])
    sb_afTf = _scoped("p1_afTf", [16, 128])
    sb_afTs = _scoped("p1_afTs", [16, 128])
    sb_g = _scoped("p1_g", [32, 1])
    sb_gT = _scoped("p1_gT", [1, 32])
    sb_gbc = _scoped("p1_gbc", [128, 16])
    sb_prod = _scoped("p1_prod", [128, 1024])
    sb_afe = _scoped("p1_afe", [128, 1024])
    sb_ase = _scoped("p1_ase", [128, 1024])

    ps_a = nc.alloc_psum_tensor("ps_a", [128, 512], F32).ap()
    ps_b = nc.alloc_psum_tensor("ps_b", [128, 1024], F32).ap()
    ps_c = nc.alloc_psum_tensor("ps_c", [128, 1024], F32).ap()
    ps_d = nc.alloc_psum_tensor("ps_d", [128, 1024], F32).ap()
    ps_x = nc.alloc_psum_tensor("ps_x", [128, 512], F32).ap()

    with nc.Block() as blk:
        @blk.sync
        def _(s):
            for w_hbm, w_sb, cols in ((wq, sb_wq, 1024), (wk, sb_wk, 1024),
                                      (wv, sb_wv, 1024)):
                s.dma_start(
                    out=w_sb[:, :].rearrange("p (i c) -> p i c", i=8),
                    in_=w_hbm[:, :].rearrange("(i p) c -> p i c", p=128),
                ).then_inc(sq, 16)
            s.dma_start(
                out=sb_wsm[:, :].rearrange("p (i c) -> p i c", i=8),
                in_=wsm[:, :].rearrange("(i p) c -> p i c", p=128),
            ).then_inc(sq, 16)
            s.dma_start(out=sb_wsmb[:, :], in_=wsmb[:, :]).then_inc(sq, 16)
            for i in range(NCH):
                if i >= 2:
                    s.wait_ge(tp, TPC * (i - 2) + 4)
                s.dma_start(out=sb_xt[:, (i % 2) * 1024:(i % 2 + 1) * 1024]
                            .rearrange("p (i c) -> p i c", i=8),
                            in_=xT[i, :, :].rearrange("(i p) c -> p i c", p=128)
                            ).then_inc(sx0 if i % 2 == 0 else sx1, 16)

        @blk.tensor
        def _(t):
            def mm(*args, inc=False, **kw):
                ins = t.matmul(*args, **kw)
                if inc:
                    ins.then_inc(tp, 1)
                return ins

            t.wait_ge(dp, 2)
            t.wait_ge(sq, 80)
            for i in range(NCH):
                bt, ba, bd = TPC * i, APC * i, DPC * i + 2
                xt = sb_xt[:, (i % 2) * 1024:(i % 2 + 1) * 1024]
                t.wait_ge(sx0 if i % 2 == 0 else sx1, 16 * (i // 2 + 1))
                if i > 0:
                    t.wait_ge(dp, DPC * i + 2)
                    t.wait_ge(ap_, APC * i)
                for ii in range(8):
                    mm(ps_a[:, 0:88], xt[:, ii * 128:(ii + 1) * 128],
                       sb_wsm[:, ii * 88:(ii + 1) * 88], start=(ii == 0), stop=False)
                mm(ps_a[:, 0:88], sb_ones_r[:, :], sb_wsmb[:, :],
                   start=False, stop=True, inc=True)
                for half in range(2):
                    for ii in range(8):
                        mm(ps_b[:, half * 512:(half + 1) * 512],
                           xt[:, ii * 128:(ii + 1) * 128],
                           sb_wk[:, ii * 1024 + half * 512:ii * 1024 + (half + 1) * 512],
                           start=(ii == 0), stop=(ii == 7),
                           inc=(half == 1 and ii == 7))
                for half in range(2):
                    for ii in range(8):
                        mm(ps_c[:, half * 512:(half + 1) * 512],
                           xt[:, ii * 128:(ii + 1) * 128],
                           sb_wv[:, ii * 1024 + half * 512:ii * 1024 + (half + 1) * 512],
                           start=(ii == 0), stop=(ii == 7),
                           inc=(half == 1 and ii == 7))
                for half in range(2):
                    for ii in range(8):
                        mm(ps_d[:, half * 512:(half + 1) * 512],
                           xt[:, ii * 128:(ii + 1) * 128],
                           sb_wq[:, ii * 1024 + half * 512:ii * 1024 + (half + 1) * 512],
                           start=(ii == 0), stop=(ii == 7),
                           inc=(half == 1 and ii == 7))
                t.wait_ge(ap_, ba + 4)
                for jt in range(8):
                    ins = t.transpose(ps_d[:, jt * 128:(jt + 1) * 128],
                                      sb_qsb[:, jt * 128:(jt + 1) * 128],
                                      sb_const[:, :])
                ins.then_inc(tp, 1)
                t.wait_ge(dp, bd + 4)
                t.transpose(ps_x[0:16, 0:128], sb_afas[:, 0:16], sb_const[:, :])
                t.transpose(ps_x[0:16, 128:256], sb_afas[:, 16:32], sb_const[:, :])
                mm(ps_x[0:32, 256:257], sb_stat[:, 64:96], sb_ones_c[:, :],
                   start=True, stop=True, inc=True)
                t.wait_ge(ap_, ba + 3)
                t.wait_ge(dp, bd + 5)
                for jt in range(8):
                    mm(ps_b[:, jt * 128:(jt + 1) * 128],
                       sb_esel[:, jt * 128:(jt + 1) * 128], sb_afTf[:, :],
                       start=(jt % 4 == 0), stop=(jt % 4 == 3))
                for jt in range(8):
                    mm(ps_c[:, jt * 128:(jt + 1) * 128],
                       sb_esel[:, jt * 128:(jt + 1) * 128], sb_afTs[:, :],
                       start=(jt % 4 == 0), stop=(jt % 4 == 3), inc=(jt == 7))
                t.wait_ge(dp, bd + 6)
                t.transpose(ps_x[0:1, 320:352], sb_g[:, 0:1],
                            sb_const[0:32, 0:32]).then_inc(tp, 1)
                t.wait_ge(dp, bd + 7)
                gT3 = sb_gT[:, :].rearrange("o (s hp odd) -> o s hp odd", s=2, hp=8)
                mm(ps_x[0:64, 384:400], sb_ones_r[:, 0:64],
                   gT3[:, :, :, 0].rearrange("o s hp -> o (s hp)"),
                   start=True, stop=True)
                mm(ps_x[64:128, 384:400], sb_ones_r[:, 0:64],
                   gT3[:, :, :, 1].rearrange("o s hp -> o (s hp)"),
                   start=True, stop=True, inc=True)
                t.wait_ge(dp, bd + 9)
                t.wait_ge(ap_, ba + 4)
                for sl in range(2):
                    vop = sb_vsb if sl == 0 else sb_vs2
                    for par in range(2):
                        for hp in range(8):
                            h = 2 * hp + par
                            pb = par * 64
                            col = (sl * 8 + hp) * 64
                            mm(ps_b[pb:pb + 64, col:col + 64],
                               vop[:, h * 64:(h + 1) * 64],
                               sb_ksb[:, h * 64:(h + 1) * 64],
                               start=(hp == 0), stop=(hp == 7),
                               inc=(sl == 1 and par == 1 and hp == 7))
                t.wait_ge(dp, bd + 11)
                for b in range(2):
                    for par in range(2):
                        for sl in range(2):
                            qop = sb_qaf if sl == 0 else sb_qas
                            for j in range(4):
                                hp = 4 * b + j
                                h = 2 * hp + par
                                pb = par * 64
                                col = hp * 128
                                mm(ps_c[pb:pb + 64, col:col + 128],
                                   sb_state[pb:pb + 64, (sl * 8 + hp) * 64:(sl * 8 + hp) * 64 + 64],
                                   qop[pb:pb + 64, hp * 128:(hp + 1) * 128],
                                   start=(sl == 0 and j == 0), stop=(sl == 1 and j == 3),
                                   inc=(b == 1 and par == 1 and sl == 1 and j == 3))

        @blk.scalar
        def _(sc):
            sc = AutoDrain(sc)
            sc.wait_ge(gp, 1)
            for i in range(NCH):
                for z in range(8 * i, 8 * (i + 1)):
                    sc.dma_start(out=bufd[z * 128:(z + 1) * 128, :],
                                 in_=sb_zero[:, :]).then_inc(szf, 16)
                bt, ba, bd = TPC * i, APC * i, DPC * i + 2
                sc.wait_ge(tp, bt + 1)
                if i > 0:
                    sc.wait_ge(dp, DPC * (i - 1) + 2 + 4)
                sc.activation(out=sb_sig[:, 0:80], in_=ps_a[:, 0:80],
                              func=AF.Sigmoid)
                sc.drain().then_inc(ap_, 1)
                sc.wait_ge(tp, bt + 2)
                if i > 0:
                    sc.wait_ge(tp, TPC * (i - 1) + 9)
                sc.activation(out=sb_ksb[:, :], in_=ps_b[:, :], func=AF.Silu)
                sc.drain().then_inc(ap_, 1)
                sc.wait_ge(tp, bt + 3)
                sc.activation(out=sb_vsb[:, :], in_=ps_c[:, :], func=AF.Silu)
                sc.drain().then_inc(ap_, 1)
                sc.wait_ge(tp, bt + 4)
                if i > 0:
                    sc.wait_ge(dp, DPC * (i - 1) + 2 + 9)
                sc.activation(out=sb_qsb[:, :], in_=ps_d[:, :], func=AF.Silu)
                sc.drain().then_inc(ap_, 1)
                sc.wait_ge(dp, bd + 2)
                sc.activation(out=sb_stat[:, 16:32], in_=sb_stat[:, 16:32], func=AF.Sqrt)
                sc.drain()
                sc.activation(out=sb_stat[:, 48:64], in_=sb_stat[:, 48:64], func=AF.Sqrt)
                sc.activation(out=sb_stat[:, 0:16], in_=sb_stat[:, 0:16],
                              func=AF.Abs, scale=3.0)
                sc.drain().then_inc(ap_, 1)
                sc.wait_ge(dp, bd + 3)
                sc.activation(out=sb_stat[:, 0:16], in_=sb_stat[:, 0:16], func=AF.Sigmoid)
                sc.drain()
                sc.activation(out=sb_stat[:, 48:64], in_=sb_stat[:, 0:16],
                              func=AF.Sigmoid, bias=sb_m5[:, :], scale=10.0)
                sc.drain().then_inc(ap_, 1)
                sc.wait_ge(dp, bd + 5)
                for _ in range(7):
                    sc.activation(out=sb_g[:, 0:1], in_=sb_g[:, 0:1], func=AF.Square)
                    sc.drain()
                sc.drain().then_inc(ap_, 1)

        @blk.vector
        def _(v):
            v = AutoDrain(v)
            for i in range(NCH):
                bt, ba, bd = TPC * i, APC * i, DPC * i + 2
                v.wait_ge(tp, bt + 1)
                v.tensor_copy(out=sb_lx0[:, i * 8:(i + 1) * 8], in_=ps_a[:, 80:88])
                v.drain().then_inc(dp, 1)
                v.wait_ge(ap_, ba + 3)
                bexp = sb_sig[:, 0:16].broadcast_to([128, 16, 64])
                k3 = sb_ksb[:, :].rearrange("p (h e) -> p h e", h=16)
                v3 = sb_vsb[:, :].rearrange("p (h e) -> p h e", h=16)
                v.tensor_tensor(out=k3, in0=k3, in1=bexp, op=ALU.mult)
                v.tensor_tensor(out=v3, in0=v3, in1=bexp, op=ALU.mult)
                v.drain()
                p3 = sb_prod[:, :].rearrange("p (h e) -> p h e", h=16)
                v.tensor_tensor(out=sb_prod[:, :], in0=sb_ksb[:, :], in1=sb_vsb[:, :],
                                op=ALU.mult)
                v.drain()
                v.reduce_sum(sb_stat[:, 0:16].rearrange("p (h o) -> p h o", o=1),
                             p3, axis=AX.X)
                v.drain()
                v.tensor_tensor(out=sb_prod[:, :], in0=sb_ksb[:, :], in1=sb_ksb[:, :],
                                op=ALU.mult)
                v.drain()
                v.reduce_sum(sb_stat[:, 16:32].rearrange("p (h o) -> p h o", o=1),
                             p3, axis=AX.X)
                v.drain()
                v.tensor_tensor(out=sb_prod[:, :], in0=sb_vsb[:, :], in1=sb_vsb[:, :],
                                op=ALU.mult)
                v.drain()
                v.reduce_sum(sb_stat[:, 48:64].rearrange("p (h o) -> p h o", o=1),
                             p3, axis=AX.X)
                v.drain().then_inc(dp, 1)
                v.wait_ge(ap_, ba + 5)
                v.tensor_scalar_add(sb_stat[:, 16:32], sb_stat[:, 16:32], 1e-8)
                v.tensor_scalar_add(sb_stat[:, 48:64], sb_stat[:, 48:64], 1e-8)
                v.drain()
                v.tensor_tensor(out=sb_stat[:, 16:32], in0=sb_stat[:, 16:32],
                                in1=sb_stat[:, 48:64], op=ALU.mult)
                v.drain()
                v.reciprocal(sb_stat[:, 32:48], sb_stat[:, 16:32])
                v.drain()
                v.tensor_tensor(out=sb_stat[:, 0:16], in0=sb_stat[:, 0:16],
                                in1=sb_stat[:, 32:48], op=ALU.mult)
                v.drain().then_inc(dp, 1)
                v.wait_ge(ap_, ba + 6)
                v.tensor_scalar(out=sb_afas[:, 0:16], in0=sb_stat[:, 0:16],
                                scalar1=0.3, scalar2=0.5, op0=ALU.mult, op1=ALU.add)
                v.drain()
                v.tensor_scalar(out=sb_afas[:, 16:32], in0=sb_afas[:, 0:16],
                                scalar1=-1.0, scalar2=1.0, op0=ALU.mult, op1=ALU.add)
                v.drain()
                v.tensor_tensor(out=sb_afas[:, 0:16], in0=sb_afas[:, 0:16],
                                in1=sb_sig[:, 48:64], op=ALU.mult)
                v.tensor_tensor(out=sb_afas[:, 16:32], in0=sb_afas[:, 16:32],
                                in1=sb_sig[:, 64:80], op=ALU.mult)
                v.tensor_scalar(out=sb_stat[:, 32:48], in0=sb_stat[:, 0:16],
                                scalar1=-0.1, scalar2=1.0, op0=ALU.mult, op1=ALU.add)
                v.drain()
                v.tensor_tensor(out=sb_stat[:, 64:80], in0=sb_sig[:, 16:32],
                                in1=sb_stat[:, 32:48], op=ALU.mult)
                v.tensor_scalar(out=sb_stat[:, 32:48], in0=sb_stat[:, 0:16],
                                scalar1=-0.05, scalar2=1.0, op0=ALU.mult, op1=ALU.add)
                v.drain()
                v.tensor_tensor(out=sb_stat[:, 80:96], in0=sb_sig[:, 32:48],
                                in1=sb_stat[:, 32:48], op=ALU.mult)
                v.tensor_tensor(out=sb_stat[:, 32:48], in0=sb_stat[:, 48:64],
                                in1=sb_stat[:, 48:64], op=ALU.mult)
                v.drain()
                v.tensor_tensor(out=sb_vs2[:, :].rearrange("p (h e) -> p h e", h=16),
                                in0=sb_vsb[:, :].rearrange("p (h e) -> p h e", h=16),
                                in1=sb_stat[:, 32:48].broadcast_to([128, 16, 64]),
                                op=ALU.mult)
                v.drain().then_inc(dp, 1)
                v.wait_ge(tp, bt + 6)
                v.tensor_copy(out=sb_afTf[:, :], in_=ps_x[0:16, 0:128])
                v.tensor_copy(out=sb_afTs[:, :], in_=ps_x[0:16, 128:256])
                v.tensor_scalar_mul(sb_g[:, 0:1], ps_x[0:32, 256:257], 1.0 / 128)
                v.drain().then_inc(dp, 1)
                v.wait_ge(ap_, ba + 7)
                v.tensor_copy(out=sb_gall[:, i:i + 1], in_=sb_g[:, 0:1])
                v.tensor_tensor(out=sb_gloc[:, i + 1:i + 2], in0=sb_gloc[:, i:i + 1],
                                in1=sb_g[:, 0:1], op=ALU.mult)
                v.drain().then_inc(dp, 1)
                v.wait_ge(tp, bt + 8)
                v.tensor_copy(out=sb_gT[:, :], in_=ps_x[0:1, 320:352])
                v.drain().then_inc(dp, 1)
                v.wait_ge(tp, bt + 9)
                v.tensor_copy(out=sb_gbc[:, :], in_=ps_x[:, 384:400])
                v.drain().then_inc(dp, 1)
                v.wait_ge(tp, bt + 7)
                v.wait_ge(ap_, ba + 4)
                if i > 0:
                    v.wait_ge(sspq, 32 * i)
                v.tensor_copy(out=sb_afe[:, :], in_=ps_b[:, :])
                v.tensor_copy(out=sb_ase[:, :], in_=ps_c[:, :])
                v.tensor_tensor(out=sb_qaf[:, :], in0=ps_d[:, :], in1=sb_afe[:, :],
                                op=ALU.mult)
                v.tensor_tensor(out=sb_qas[:, :], in0=ps_d[:, :], in1=sb_ase[:, :],
                                op=ALU.mult)
                v.drain().then_inc(dp, 1)
                v.wait_ge(tp, bt + 10)
                if i == 0:
                    v.wait_ge(gp, 2)
                v.tensor_tensor(out=sb_state[:, :].rearrange("p (cb e) -> p cb e", cb=16),
                                in0=sb_state[:, :].rearrange("p (cb e) -> p cb e", cb=16),
                                in1=sb_gbc[:, :].broadcast_to([128, 16, 64]),
                                op=ALU.mult)
                v.drain()
                v.drain().then_inc(dp, 1)
                v.tensor_tensor(out=sb_state[:, :], in0=sb_state[:, :],
                                in1=ps_b[:, :], op=ALU.add)
                v.drain()
                v.drain().then_inc(dp, 1)
                v.wait_ge(tp, bt + 11)
                if i > 0:
                    v.wait_ge(sspo, 16 * i)
                v.tensor_copy(out=sb_ot[:, :], in_=ps_c[:, :])
                v.drain().then_inc(dp, 1)

        @blk.gpsimd
        def _(g):
            g.memset(sb_state[:, :], 0.0)
            g.drain().then_inc(gp, 1)
            for i in range(NCH):
                bd = DPC * i + 2
                g.wait_ge(dp, bd + 9)
                g.dma_start(out=qfsp[i, :, :].rearrange("(jt p) t -> p jt t", p=128),
                            in_=sb_qaf[:, :].rearrange("p (jt t) -> p jt t", jt=8)
                            ).then_inc(sspq, 16)
                g.dma_start(out=qssp[i, :, :].rearrange("(jt p) t -> p jt t", p=128),
                            in_=sb_qas[:, :].rearrange("p (jt t) -> p jt t", jt=8)
                            ).then_inc(sspq, 16)
                g.wait_ge(dp, bd + 12)
                g.dma_start(out=otsp[i, :, :].rearrange("(jt p) t -> p jt t", p=128),
                            in_=sb_ot[:, :].rearrange("p (jt t) -> p jt t", jt=8)
                            ).then_inc(sspo, 16)

    with nc.Block() as blk:
        @blk.gpsimd
        def _(g):
            g.wait_ge(dp, 2 + DPC * NCH)
            g.wait_ge(sspq, 32 * NCH)
            g.wait_ge(sspo, 16 * NCH)
            for odd in range(2):
                g.dma_start(
                    out=AP(tensor=cc1i.tensor, offset=cc1i.offset + odd * 4097,
                           ap=[[64, 64], [16 * 4097, 2], [2 * 4097, 8], [1, 64]]),
                    in_=sb_state[odd * 64:(odd + 1) * 64, :]
                    .rearrange("p (s hp e) -> p s hp e", s=2, hp=8),
                ).then_inc(scw, 16)
            g.dma_start(out=cc1i[:, 4096:4097], in_=sb_gloc[:, NCH:NCH + 1],
                        allow_slow_non_contiguous=True).then_inc(scw, 16)
            g.wait_ge(scw, 48)
            g.collective_compute(
                "AllGather", ALU.bypass, replica_groups=[list(range(NCORE))],
                ins=[cc1i.tensor.ap().opt()], outs=[cc1o.tensor.ap().opt()],
            ).then_inc(ccs, 1)
            g.wait_ge(ccs, 1)

        if debug:
            @blk.scalar
            def _(sc):
                sc.wait_ge(ccs, 1)
                sc.dma_start(out=dbg_state[:, :], in_=sb_state[:, :]).then_inc(sq, 16)
                sc.dma_start(out=dbg_gall[:, :], in_=sb_gall[:, :]).then_inc(sq, 16)
                sc.dma_start(out=dbg_gloc[:, :], in_=sb_gloc[:, :]).then_inc(sq, 16)
                sc.dma_start(out=dbg_lx0[:, :], in_=sb_lx0[:, :]).then_inc(sq, 16)
                sc.wait_ge(sq, 80 + 64)
                for half in range(2):
                    sc.dma_start(out=sb_wq[:, 0:4097],
                                 in_=cc1o[half * 128:(half + 1) * 128, :]).then_inc(sq, 16)
                    sc.wait_ge(sq, 160 + 32 * half)
                    sc.dma_start(out=dbg_cc1o[half * 128:(half + 1) * 128, :],
                                 in_=sb_wq[:, 0:4097]).then_inc(sq, 16)
                    sc.wait_ge(sq, 176 + 32 * half)

    for g_ in reversed(_p1_guards):
        g_.__exit__(None, None, None)

    _p3_guards = []

    def _scp3(name, shape, dtype=F32):
        g_ = nc.sbuf_tensor(name, shape, dtype)
        h_ = g_.__enter__()
        _p3_guards.append(g_)
        return h_.ap()

    sb_wo = _scp3("p3_wo", [128, 8 * 1024])
    sb_x1 = _scp3("p3_x1", [128, NCH * RW])
    sb_wor = _scp3("p3_wor", [128, 64])
    sb_sbc = _scp3("p3_sbc", [128, 8])
    sb_rt = _scp3("p3_rt", [128, 64])

    with nc.Block() as blk:
        @blk.sync
        def _(s):
            s.wait_ge(ccs, 1)
            s.dma_start(out=sb_wo[:, :].rearrange("p (i c) -> p i c", i=8),
                        in_=woT[:, :].rearrange("(i p) c -> p i c", p=128)
                        ).then_inc(syc0, 16)
            s.dma_start(out=sb_rt[:, :].rearrange("p (i c) -> p i c", i=8),
                        in_=rT[:, :].rearrange("(i p) c -> p i c", p=128)
                        ).then_inc(syc0, 16)

        @blk.tensor
        def _(t):
            t.wait_ge(syc0, 32)
            for jt in range(8):
                for ii in range(8):
                    t.matmul(ps_a[:, jt * 8:(jt + 1) * 8],
                             sb_wo[:, ii * 1024 + jt * 128:ii * 1024 + (jt + 1) * 128],
                             sb_rt[:, ii * 8:(ii + 1) * 8],
                             start=(jt % 8 == 0 and ii == 0),
                             stop=(jt == 7 and ii == 7))
            for ii in range(8):
                t.matmul(ps_x[0:1, 0:8], sb_ones_c[:, :], sb_rt[:, ii * 8:(ii + 1) * 8],
                         start=(ii == 0), stop=(ii == 7)).then_inc(set_, 1) if ii == 7 else \
                    t.matmul(ps_x[0:1, 0:8], sb_ones_c[:, :], sb_rt[:, ii * 8:(ii + 1) * 8],
                             start=(ii == 0), stop=(ii == 7))

        @blk.vector
        def _(v):
            v = AutoDrain(v)
            v.wait_ge(set_, 1)
            v.tensor_copy(out=sb_wor[:, :], in_=ps_a[:, 0:64])
            v.tensor_copy(out=sb_rt[0:1, 0:8], in_=ps_x[0:1, 0:8])
            v.drain().then_inc(sed, 1)

        @blk.tensor
        def _(t):
            t.wait_ge(sed, 1)
            t.matmul(ps_x[:, 8:16], sb_ones_r[:, :], sb_rt[0:1, 0:8],
                     start=True, stop=True).then_inc(set_, 1)

        @blk.vector
        def _(v):
            v = AutoDrain(v)
            v.wait_ge(set_, 2)
            v.tensor_copy(out=sb_sbc[:, :], in_=ps_x[:, 8:16])
            v.drain().then_inc(sed, 1)

        @blk.sync
        def _(s):
            s.wait_ge(set_, 2)
            s.dma_start(out=sb_wo[:, :].rearrange("p (i c) -> p i c", i=8),
                        in_=wo[:, :].rearrange("(i p) c -> p i c", p=128)
                        ).then_inc(syc0, 16)

    _p3a_guards = []

    def _sc3(name, shape, dtype=F32):
        g_ = nc.sbuf_tensor(name, shape, dtype)
        h_ = g_.__enter__()
        _p3a_guards.append(g_)
        return h_.ap()

    sb_sent = _sc3("p3_sent", [32, 4096])
    sb_sj = _sc3("p3_sj", [32, 4096])
    sb_gj = _sc3("p3_gj", [32, 8])
    sb_al = _sc3("p3_al", [32, 8])
    sb_alt = _sc3("p3_alt", [32, 8])
    sb_segp = _sc3("p3_segp", [32, 8])
    sb_sentT = _sc3("p3_sentT", [128, 1024])
    sb_qgf = _sc3("p3_qgf", [128, 2048])
    sb_qgs = _sc3("p3_qgs", [128, 2048])
    sb_glP0 = _sc3("p3_glP0", [1, 512])
    sb_glbc = _sc3("p3_glbc", [128, 32])
    sb_corr = _sc3("p3_corr", [128, 2048])

    with nc.Block() as blk:
        @blk.gpsimd
        def _(g):
            g = AutoDrain(g)
            g.wait_ge(ccs, 1)
            g.memset(sb_x1[:, :], 0.0)
            g.memset(sb_sent[:, :], 0.0)
            g.dma_start(out=sb_segp[:, :], in_=segp[:, :]).then_inc(sg2, 16)
            g.dma_start(out=sb_gj[:, :],
                        in_=AP(tensor=cc1o.tensor, offset=cc1o.offset + 4096,
                               ap=[[4097, 32], [32 * 4097, 8]]),
                        allow_slow_non_contiguous=True).then_inc(sg2, 16)
            g.dma_start(out=sb_glP0[:, :],
                        in_=sb_gloc[:, 1:NCH + 1],
                        allow_slow_non_contiguous=True).then_inc(sg2, 16)
            for j in range(NCORE):
                g.wait_ge(sed, 2 + 2 * j)
                g.dma_start(out=sb_sj[:, :],
                            in_=cc1o[32 * j:32 * (j + 1), 0:4096]).then_inc(ssj, 16)

        @blk.vector
        def _(v):
            v = AutoDrain(v)
            v.wait_ge(sg2, 48)
            v.tensor_tensor(out=sb_al[:, :], in0=sb_gj[:, :], in1=sb_segp[:, :],
                            op=ALU.mult)
            v.tensor_tensor(out=sb_al[:, :], in0=sb_al[:, :], in1=sb_segp[:, :],
                            op=ALU.subtract)
            v.tensor_scalar_add(sb_al[:, :], sb_al[:, :], 1.0)
            for j in range(NCORE):
                v.wait_ge(ssj, 16 * (j + 1))
                v.tensor_scalar(out=sb_sj[:, :], in0=sb_sj[:, :],
                                scalar1=sb_segp[:, j:j + 1], scalar2=None, op0=ALU.mult)
                v.scalar_tensor_tensor(out=sb_sent[:, :], in0=sb_sent[:, :],
                                       scalar=sb_al[:, j:j + 1], in1=sb_sj[:, :],
                                       op0=ALU.mult, op1=ALU.add)
                v.drain().then_inc(sed, 2)

        @blk.gpsimd
        def _(g):
            g.wait_ge(sed, 2 + 2 * NCORE)
            g.dma_start(out=sentb[:, :], in_=sb_sent[:, :]).then_inc(ssb, 16)
            g.wait_ge(ssb, 16)
            for par in range(2):
                g.dma_start(
                    out=sb_sentT[par * 64:(par + 1) * 64, :]
                    .rearrange("p (s hp e) -> p s hp e", s=2, hp=8),
                    in_=AP(tensor=sentb.tensor, offset=sentb.offset + par * 4096,
                           ap=[[64, 64], [16 * 4096, 2], [2 * 4096, 8], [1, 64]]),
                ).then_inc(ssb, 16)

    SEDB = 18
    with nc.Block() as blk:
        @blk.sync
        def _(s):
            for hp in range(8):
                if hp > 0:
                    s.wait_ge(set_, 2 + 2 * hp)
                s.dma_start(out=sb_qgf[:, :].rearrange("p (c t) -> p c t", c=16),
                            in_=qfsp[:, 128 * hp:128 * (hp + 1), :]
                            .rearrange("c p t -> p c t")).then_inc(syc1, 16)
                s.dma_start(out=sb_qgs[:, :].rearrange("p (c t) -> p c t", c=16),
                            in_=qssp[:, 128 * hp:128 * (hp + 1), :]
                            .rearrange("c p t -> p c t")).then_inc(syc1, 16)

        @blk.tensor
        def _(t):
            t.wait_ge(ssb, 48)
            for hp in range(8):
                rows = [2 * hp, 2 * hp + 1, 16 + 2 * hp, 16 + 2 * hp + 1]
                if hp > 0:
                    t.wait_ge(sed, SEDB + 3 * (hp - 1) + 1)
                for k, (pb, scol) in enumerate(((0, 0), (64, 0), (0, 16), (64, 16))):
                    ins = t.matmul(ps_x[pb:pb + 64, 32 + scol:48 + scol],
                                   sb_ones_r[:, 0:64],
                                   sb_glP0[:, rows[k] * 16:(rows[k] + 1) * 16],
                                   start=True, stop=True)
                    if k == 3:
                        ins.then_inc(set_, 1)
                t.wait_ge(sed, SEDB + 3 * hp + 2)
                for par in range(2):
                    pb = par * 64
                    for sl in range(2):
                        qg = sb_qgf if sl == 0 else sb_qgs
                        for tq in range(4):
                            tgt = ps_b if tq < 2 else ps_c
                            col = (tq % 2) * 512
                            ins = t.matmul(
                                tgt[pb:pb + 64, col:col + 512],
                                sb_sentT[pb:pb + 64,
                                         (sl * 8 + hp) * 64:(sl * 8 + hp) * 64 + 64],
                                qg[pb:pb + 64, tq * 512:(tq + 1) * 512],
                                start=(sl == 0), stop=(sl == 1))
                            if par == 1 and sl == 1 and tq == 3:
                                ins.then_inc(set_, 1)

        @blk.vector
        def _(v):
            v = AutoDrain(v)
            for hp in range(8):
                v.wait_ge(set_, 2 + 2 * hp + 1)
                v.wait_ge(syc1, 32 * (hp + 1))
                v.tensor_copy(out=sb_glbc[:, :], in_=ps_x[:, 32:64])
                v.drain().then_inc(sed, 1)
                v.tensor_tensor(out=sb_qgf[:, :].rearrange("p (c t) -> p c t", c=16),
                                in0=sb_qgf[:, :].rearrange("p (c t) -> p c t", c=16),
                                in1=sb_glbc[:, 0:16].broadcast_to([128, 16, 128]),
                                op=ALU.mult)
                v.tensor_tensor(out=sb_qgs[:, :].rearrange("p (c t) -> p c t", c=16),
                                in0=sb_qgs[:, :].rearrange("p (c t) -> p c t", c=16),
                                in1=sb_glbc[:, 16:32].broadcast_to([128, 16, 128]),
                                op=ALU.mult)
                v.drain().then_inc(sed, 1)
                v.wait_ge(set_, 2 + 2 * hp + 2)
                if hp > 0:
                    v.wait_ge(sco, 16 * hp)
                v.tensor_copy(out=sb_corr[:, 0:1024], in_=ps_b[:, :])
                v.tensor_copy(out=sb_corr[:, 1024:2048], in_=ps_c[:, :])
                v.drain().then_inc(sed, 1)

        @blk.gpsimd
        def _(g):
            g.wait_ge(sspo, 16 * NCH)
            for hp in range(8):
                g.wait_ge(sed, SEDB + 3 * hp + 3)
                g.dma_start(out=otsp[:, 128 * hp:128 * (hp + 1), :]
                            .rearrange("c p t -> p c t"),
                            in_=sb_corr[:, :].rearrange("p (c t) -> p c t", c=16),
                            accum_op=ALU.add).then_inc(sco, 16)

    for g_ in reversed(_p3a_guards):
        g_.__exit__(None, None, None)

    _p3b_guards = []

    def _scb(name, shape, dtype=F32):
        g_ = nc.sbuf_tensor(name, shape, dtype)
        h_ = g_.__enter__()
        _p3b_guards.append(g_)
        return h_.ap()

    sb_otc = _scb("b_otc", [128, 2 * 1024])
    sb_x0c = _scb("b_x0c", [128, 2 * 1024])
    sb_xp = _scb("b_xp", [128, 1024])
    sb_sq = _scb("b_sq", [128, 1024])
    sb_rs = _scb("b_rs", [128, 16])
    sb_l8 = _scb("b_l8", [128, 8])
    sb_l8b = _scb("b_l8b", [128, 8])
    sb_ex = _scb("b_ex", [128, 8])
    sb_ohw = _scb("b_ohw", [128, 8])
    sb_oh = _scb("b_oh", [128, 128])
    sb_dest = _scb("b_dest", [128, NCH], I32)
    sb_cnt8 = _scb("b_cnt8", [8, 1])
    sb_cc2 = _scb("b_cc2", [8, 8])
    sb_cntp = _scb("b_cntp", [8, 1])
    sb_thr8 = _scb("b_thr8", [8, 1])
    sb_throw = _scb("b_throw", [1, 8])

    SB3 = 42
    TB3 = 18
    SY0B, SY1B = 48, 256

    with nc.Block() as blk:
        @blk.sync
        def _(s):
            s.wait_ge(sco, 128)
            for tau in range(NCH):
                sem = syc0 if tau % 2 == 0 else syc1
                base = SY0B if tau % 2 == 0 else SY1B
                if tau >= 2:
                    s.wait_ge(set_, TB3 + 2 * (tau - 2) + 1)
                    s.wait_ge(sed, SB3 + 5 * (tau - 2) + 3)
                half = (tau % 2) * 1024
                s.dma_start(out=sb_otc[:, half:half + 1024]
                            .rearrange("p (j t) -> p j t", j=8),
                            in_=otsp[tau, :, :].rearrange("(j p) t -> p j t", p=128)
                            ).then_inc(sem, 16)
                s.dma_start(out=sb_x0c[:, half:half + 1024],
                            in_=x0r[tau * 128:(tau + 1) * 128, :]).then_inc(sem, 16)

        @blk.tensor
        def _(t):
            for tau in range(NCH):
                sem = syc0 if tau % 2 == 0 else syc1
                base = SY0B if tau % 2 == 0 else SY1B
                half = (tau % 2) * 1024
                t.wait_ge(sem, base + 32 * (tau // 2 + 1))
                if tau > 0:
                    t.wait_ge(sed, SB3 + 5 * (tau - 1) + 3)
                for jt in range(8):
                    ot_t = sb_otc[:, half + jt * 128:half + (jt + 1) * 128]
                    for hh in range(2):
                        t.matmul(ps_d[:, hh * 512:(hh + 1) * 512], ot_t,
                                 sb_wo[:, jt * 1024 + hh * 512:jt * 1024 + (hh + 1) * 512],
                                 start=(jt == 0), stop=(jt == 7))
                    ins = t.matmul(ps_a[:, 0:8], ot_t, sb_wor[:, jt * 8:(jt + 1) * 8],
                                   start=(jt == 0), stop=(jt == 7))
                    if jt == 7:
                        ins.then_inc(set_, 1)
                t.wait_ge(sed, SB3 + 5 * tau + 4)
                for tp_ in range(tau + 1):
                    lhs = sb_u128 if tp_ == tau else sb_o128
                    t.matmul(ps_a[:, 128:136], lhs,
                             sb_oh[:, tp_ * 8:(tp_ + 1) * 8],
                             start=(tp_ == 0), stop=(tp_ == tau))
                ins = t.matmul(ps_x[0:8, 0:1], sb_oh[:, tau * 8:(tau + 1) * 8],
                               sb_ones_c[:, :], start=(tau == 0), stop=(tau == NCH - 1),
                               skip_group_check=True)
                ins.then_inc(set_, 1)

        @blk.scalar
        def _(sc):
            sc = AutoDrain(sc)
            for tau in range(NCH):
                AB = 3 * tau
                sc.wait_ge(sed, SB3 + 5 * tau + 1)
                sc.activation(out=sb_sq[:, :], in_=sb_xp[:, :], func=AF.Square,
                              accum_out=sb_rs[:, 1:2])
                sc.drain().then_inc(sea, 1)
                sc.wait_ge(sed, SB3 + 5 * tau + 2)
                sc.activation(out=sb_rs[:, 4:5], in_=sb_rs[:, 3:4], func=AF.Sqrt,
                              bias=sb_eps[:, :])
                sc.drain().then_inc(sea, 1)
                sc.wait_ge(sed, SB3 + 5 * tau + 3)
                sc.activation(out=sb_ex[:, :], in_=sb_l8[:, :], func=AF.Exp,
                              bias=sb_rs[:, 7:8], accum_out=sb_rs[:, 8:9])
                sc.drain().then_inc(sea, 1)

        @blk.vector
        def _(v):
            v = AutoDrain(v)
            for tau in range(NCH):
                sem = syc0 if tau % 2 == 0 else syc1
                base = SY0B if tau % 2 == 0 else SY1B
                half = (tau % 2) * 1024
                AB = 3 * tau
                v.wait_ge(set_, TB3 + 2 * tau + 1)
                v.wait_ge(sem, base + 32 * (tau // 2 + 1))
                if tau > 0:
                    v.wait_ge(sea, AB - 3 + 1)
                v.tensor_tensor(out=sb_xp[:, :], in0=ps_d[:, :],
                                in1=sb_x0c[:, half:half + 1024], op=ALU.add)
                v.reduce_sum(sb_rs[:, 0:1], sb_xp[:, :], axis=AX.X)
                v.tensor_scalar_mul(sb_rs[:, 0:1], sb_rs[:, 0:1], 1.0 / 1024)
                v.drain().then_inc(sed, 1)
                v.wait_ge(sea, AB + 1)
                v.tensor_tensor(out=sb_rs[:, 2:3], in0=sb_rs[:, 0:1],
                                in1=sb_rs[:, 0:1], op=ALU.mult)
                v.tensor_scalar_mul(sb_rs[:, 3:4], sb_rs[:, 1:2], 1.0 / 1024)
                v.tensor_tensor(out=sb_rs[:, 3:4], in0=sb_rs[:, 3:4],
                                in1=sb_rs[:, 2:3], op=ALU.subtract)
                v.drain().then_inc(sed, 1)
                v.wait_ge(sea, AB + 2)
                v.reciprocal(sb_rs[:, 5:6], sb_rs[:, 4:5])
                v.tensor_scalar(out=sb_x1[:, tau * RW:tau * RW + 1024], in0=sb_xp[:, :],
                                scalar1=sb_rs[:, 0:1], scalar2=sb_rs[:, 5:6],
                                op0=ALU.subtract, op1=ALU.mult)
                v.tensor_tensor(out=sb_l8[:, :], in0=ps_a[:, 0:8],
                                in1=sb_lx0[:, tau * 8:(tau + 1) * 8], op=ALU.add)
                v.tensor_scalar(out=sb_l8b[:, :], in0=sb_sbc[:, :],
                                scalar1=sb_rs[:, 0:1], scalar2=None, op0=ALU.mult)
                v.tensor_tensor(out=sb_l8[:, :], in0=sb_l8[:, :], in1=sb_l8b[:, :],
                                op=ALU.subtract)
                v.tensor_scalar(out=sb_l8[:, :], in0=sb_l8[:, :],
                                scalar1=sb_rs[:, 5:6], scalar2=None, op0=ALU.mult)
                v.reduce_max(sb_rs[:, 6:7], sb_l8[:, :], axis=AX.X)
                v.tensor_scalar_mul(sb_rs[:, 7:8], sb_rs[:, 6:7], -1.0)
                v.drain().then_inc(sed, 1)
                v.wait_ge(sea, AB + 3)
                v.reciprocal(sb_rs[:, 9:10], sb_rs[:, 8:9])
                v.tensor_scalar(out=sb_ex[:, :], in0=sb_ex[:, :],
                                scalar1=sb_rs[:, 9:10], scalar2=None, op0=ALU.mult)
                v.reduce_max(sb_rs[:, 10:11], sb_ex[:, :], axis=AX.X)
                v.tensor_copy(out=sb_x1[:, tau * RW + 1024:tau * RW + 1025],
                              in_=sb_rs[:, 10:11])
                v.tensor_scalar_add(sb_x1[:, tau * RW + 1025:tau * RW + 1026],
                                    sb_iota_c[:, :], float(tau * 128))
                v.tensor_scalar(out=sb_ohw[:, :], in0=sb_ex[:, :],
                                scalar1=sb_rs[:, 10:11], scalar2=None, op0=ALU.is_equal)
                v.scalar_tensor_tensor(out=sb_ohw[:, :], in0=sb_ohw[:, :],
                                       scalar=-16.0, in1=sb_iota8[:, :],
                                       op0=ALU.mult, op1=ALU.add)
                v.tensor_reduce(sb_rs[:, 11:12], sb_ohw[:, :], axis=AX.X, op=ALU.min)
                v.tensor_scalar_add(sb_rs[:, 11:12], sb_rs[:, 11:12], 16.0)
                v.tensor_scalar(out=sb_oh[:, tau * 8:(tau + 1) * 8], in0=sb_iota8[:, :],
                                scalar1=sb_rs[:, 11:12], scalar2=None, op0=ALU.is_equal)
                v.drain().then_inc(sed, 1)
                v.wait_ge(set_, TB3 + 2 * tau + 2)
                v.tensor_tensor(out=sb_ohw[:, :], in0=ps_a[:, 128:136],
                                in1=sb_adder[:, :], op=ALU.add)
                v.tensor_tensor(out=sb_ohw[:, :], in0=sb_ohw[:, :],
                                in1=sb_oh[:, tau * 8:(tau + 1) * 8], op=ALU.mult)
                v.reduce_sum(sb_rs[:, 12:13], sb_ohw[:, :], axis=AX.X)
                v.tensor_copy(out=sb_dest[:, tau:tau + 1], in_=sb_rs[:, 12:13])
                v.drain().then_inc(sed, 1)

        @blk.gpsimd
        def _(g):
            g.wait_ge(szf, 16 * (E * REG // 128))
            for tau in range(NCH):
                g.wait_ge(sed, SB3 + 5 * tau + 5)
                g.indirect_dma_start(
                    out=bufd[:, :],
                    out_offset=IndirectOffsetOnAxis(sb_dest[:, tau:tau + 1], 0),
                    in_=sb_x1[:, tau * RW:(tau + 1) * RW], in_offset=None,
                ).then_inc(ssc, 16)

    SB4 = SB3 + 5 * NCH
    TB4 = TB3 + 2 * NCH
    with nc.Block() as blk:
        @blk.vector
        def _(v):
            v = AutoDrain(v)
            v.wait_ge(set_, TB4)
            v.tensor_copy(out=sb_cnt8[:, :], in_=ps_x[0:8, 0:1])
            v.drain().then_inc(sed, 1)
            v.wait_ge(set_, TB4 + 1)
            v.tensor_copy(out=sb_cntrow[:, :], in_=ps_x[0:1, 8:16])
            v.drain().then_inc(sed, 1)
            v.wait_ge(set_, TB4 + 2)
            v.tensor_scalar(out=sb_thr8[:, :], in0=ps_x[0:8, 16:17],
                            scalar1=-1.0, scalar2=float(CAP), op0=ALU.mult, op1=ALU.add)
            v.drain().then_inc(sed, 1)
            v.wait_ge(set_, TB4 + 3)
            v.tensor_copy(out=sb_throw[:, :], in_=ps_x[0:1, 24:32])
            v.drain().then_inc(sed, 1)
            v.wait_ge(set_, TB4 + 4)
            v.tensor_copy(out=sb_thrbc[:, :], in_=ps_b[:, 0:8])
            v.tensor_copy(out=sb_cntbc[:, :], in_=ps_b[:, 8:16])
            v.tensor_copy(out=sb_cti[:, :], in_=sb_cntrow[:, :])
            v.tensor_scalar(out=sb_cct[:, :], in0=sb_cti[:, :], scalar1=9,
                            scalar2=None, op0=ALU.arith_shift_right)
            v.tensor_scalar(out=sb_ctt[:, :], in0=sb_cct[:, :], scalar1=9,
                            scalar2=None, op0=ALU.arith_shift_left)
            v.tensor_tensor(out=sb_ctt[:, :], in0=sb_cti[:, :], in1=sb_ctt[:, :],
                            op=ALU.subtract)
            v.tensor_scalar_add(sb_ctt[:, :], sb_ctt[:, :], 127)
            v.tensor_scalar(out=sb_ctt[:, :], in0=sb_ctt[:, :], scalar1=7,
                            scalar2=None, op0=ALU.arith_shift_right)
            v.drain().then_inc(sed, 1)
            v.tensor_copy(out=sb_nff[:, :], in_=sb_cct[:, :])
            v.tensor_scalar(
                out=sb_sbrow[:, 0:32].rearrange("o (e t) -> o e t", e=8),
                in0=sb_nff[:, :].rearrange("o (e t) -> o e t", t=1)
                .broadcast_to([1, 8, 4]),
                scalar1=512.0, scalar2=None, op0=ALU.mult)
            v.tensor_tensor(out=sb_sbrow[:, 0:32], in0=sb_sbrow[:, 0:32],
                            in1=sb_io4r[:, :], op=ALU.add)
            v.drain()
            v.tensor_tensor(out=sb_sbrow[:, 32:64], in0=sb_sbrow[:, 0:32],
                            in1=sb_e2k[:, :], op=ALU.add)
            v.drain().then_inc(sed, 1)
            v.wait_ge(set_, TB4 + 5)
            v.tensor_copy(out=sb_sbase[:, :], in_=ps_b[:, 16:80])
            v.tensor_scalar(out=sb_tidx[:, :], in0=sb_sbase[:, 32:64],
                            scalar1=sb_iota_c[:, :], scalar2=None, op0=ALU.add)
            v.drain().then_inc(sed, 1)

        @blk.tensor
        def _(t):
            t.wait_ge(sed, SB4 + 1)
            t.transpose(ps_x[0:1, 8:16], sb_cnt8[:, :], sb_const[0:8, 0:8]
                        ).then_inc(set_, 1)
            t.wait_ge(s2o, 32)
            t.wait_ge(ccs, 2)
            t.matmul(ps_x[0:8, 16:17], sb_cc2[:, :], sb_cntp[:, :],
                     start=True, stop=True).then_inc(set_, 1)
            t.wait_ge(sed, SB4 + 3)
            t.transpose(ps_x[0:1, 24:32], sb_thr8[:, :], sb_const[0:8, 0:8]
                        ).then_inc(set_, 1)
            t.wait_ge(sed, SB4 + 4)
            t.matmul(ps_b[:, 0:8], sb_ones_r[:, :], sb_throw[:, :],
                     start=True, stop=True)
            t.matmul(ps_b[:, 8:16], sb_ones_r[:, :], sb_cntrow[:, :],
                     start=True, stop=True).then_inc(set_, 1)
            t.wait_ge(sed, SB4 + 6)
            t.matmul(ps_b[:, 16:80], sb_ones_r[:, :], sb_sbrow[:, :],
                     start=True, stop=True).then_inc(set_, 1)

        @blk.gpsimd
        def _(g):
            g.wait_ge(sed, SB4 + 2)
            g.dma_start(out=cc2i[:, :], in_=sb_cntrow[:, :]).then_inc(s2i, 16)
            g.wait_ge(s2i, 16)
            g.collective_compute(
                "AllGather", ALU.bypass, replica_groups=[list(range(NCORE))],
                ins=[cc2i.tensor.ap().opt()], outs=[cc2o.tensor.ap().opt()],
            ).then_inc(ccs, 1)
            g.wait_ge(ccs, 2)
            g.dma_start(out=sb_cc2[:, :], in_=cc2o[:, :]).then_inc(s2o, 16)
            g.dma_start(out=sb_cntp[:, :], in_=cntp[:, :]).then_inc(s2o, 16)

    if debug:
        with nc.Block() as blk:
            @blk.scalar
            def _(sc):
                sc.wait_ge(sed, SB4 + 5)
                sc.wait_ge(ssc, 16 * NCH)
                sc.dma_start(out=dbg_x1[:, :], in_=sb_x1[:, :]).then_inc(sq, 16)
                sc.dma_start(out=dbg_oh[:, :], in_=sb_oh[:, :]).then_inc(sq, 16)
                sc.dma_start(out=dbg_dest[:, :], in_=sb_dest[:, :]).then_inc(sq, 16)
                sc.dma_start(out=dbg_rows[0:1, 0:8], in_=sb_cntrow[:, :]).then_inc(sq, 16)
                sc.dma_start(out=dbg_rows[1:2, 0:8], in_=sb_throw[:, :]).then_inc(sq, 16)
                sc.wait_ge(sq, 208 + 80)

    for g_ in reversed(_p3b_guards):
        g_.__exit__(None, None, None)
    for g_ in reversed(_p3_guards):
        g_.__exit__(None, None, None)

    sb_w1b = nc.alloc_sbuf_tensor("p5_w1b", [128, 2 * 4096], F32).ap()
    sb_w2b = nc.alloc_sbuf_tensor("p5_w2b", [128, 2 * 4096], F32).ap()
    sb_hT = nc.alloc_sbuf_tensor("p5_hT", [128, 16384], F32).ap()
    sb_bfin = nc.alloc_sbuf_tensor("p5_bfin", [128, 4 * RW], F32).ap()
    sb_xtf = nc.alloc_sbuf_tensor("p5_xtf", [128, 4096], F32).ap()
    sb_eoT = nc.alloc_sbuf_tensor("p5_eoT", [128, 4096], F32).ap()
    sb_wb = nc.alloc_sbuf_tensor("p5_wb", [128, 1024], F32).ap()
    sb_wb2 = nc.alloc_sbuf_tensor("p5_wb2", [128, 1024], F32).ap()
    sb_sq2 = nc.alloc_sbuf_tensor("p5_sq2", [128, 1024], F32).ap()
    sb_b1e = nc.alloc_sbuf_tensor("p5_b1e", [128, 32], F32).ap()
    sb_b2e = nc.alloc_sbuf_tensor("p5_b2e", [128, 8], F32).ap()
    sb_r5 = nc.alloc_sbuf_tensor("p5_r5", [128, 16], F32).ap()
    sb_dsw = nc.alloc_sbuf_tensor("p5_dsw", [128, 1], I32).ap()

    sw1a = nc.alloc_semaphore("sw1a")
    sw1b = nc.alloc_semaphore("sw1b")
    sw2a = nc.alloc_semaphore("sw2a")
    sw2b = nc.alloc_semaphore("sw2b")
    sbf = nc.alloc_semaphore("sbf")
    sbf2 = nc.alloc_semaphore("sbf2")
    sbe = nc.alloc_semaphore("sbe")
    st5 = nc.alloc_semaphore("st5")
    sd5 = nc.alloc_semaphore("sd5")
    sa5 = nc.alloc_semaphore("sa5")
    swb = nc.alloc_semaphore("swb")
    stl1p = [nc.alloc_semaphore("stl1a"), nc.alloc_semaphore("stl1b")]
    stl2p = [nc.alloc_semaphore("stl2a"), nc.alloc_semaphore("stl2b")]
    sfin = nc.alloc_semaphore("sfin")
    sfin2 = nc.alloc_semaphore("sfin2")

    BLOCKS = []
    for e_ in range(E):
        for cc_ in range(4):
            BLOCKS.append(("F", e_, cc_))
        for tt_ in range(4):
            BLOCKS.append(("T", e_, tt_))
    NBLK = len(BLOCKS)
    S_PE = {"F": 24, "T": 18}
    S_DV = {"F": 17, "T": 5}
    S_AC = {"F": 16, "T": 10}
    NSC = {"F": 4, "T": 1}
    st_base = np.cumsum([0] + [S_PE[k] for k, _, _ in BLOCKS]).tolist()
    sd_base = np.cumsum([0] + [S_DV[k] for k, _, _ in BLOCKS]).tolist()
    sa_base = np.cumsum([0] + [S_AC[k] for k, _, _ in BLOCKS]).tolist()
    sc_base = np.cumsum([0] + [NSC[k] for k, _, _ in BLOCKS]).tolist()
    nF_upto = np.cumsum([1 if k == "F" else 0 for k, _, _ in BLOCKS]).tolist()
    nT_upto = np.cumsum([1 if k == "T" else 0 for k, _, _ in BLOCKS]).tolist()

    def w1_stage(kind, fb):
        return (4 if kind == "F" else 1) + 1 + fb

    def w2_stage(kind, fb):
        return (12 if kind == "F" else 9) + 1 + fb

    NJ = {"F": 4, "T": 1}
    TW = {"F": 512, "T": 128}
    TSLOT = ["a", "b", "o0", "o1", "o2", "a", "b", "o0"]

    def tslot(base_sb, fb):
        sl = TSLOT[fb]
        if sl == "a":
            return base_sb, 0
        if sl == "b":
            return base_sb, 4096
        return sb_hT, 4096 + int(sl[1]) * 4096

    eo_map = [(0, 0), (1, 0), (1, 512), (2, 0), (2, 512), (3, 0), (3, 512), (4, 0)]

    def eo_tile(dt_):
        ten = (ps_a, ps_b, ps_b, ps_c, ps_c, ps_d, ps_d, ps_x)[dt_]
        col = (0, 0, 512, 0, 512, 0, 512, 0)[dt_]
        return ten, col

    bufd3 = bufd[:, :].rearrange("(n p) w -> n p w", p=128)

    with nc.Block() as blk:
        @blk.sync
        def _(s):
            s.wait_ge(ssc, 16 * NCH)
            s.wait_ge(sed, SB4 + 7)
            s.wait_ge(set_, TB4 + 5)
            rgf = s.alloc_register("p5s_nf")
            rgt_ = s.alloc_register("p5s_nt")
            for b, (kind, e, ii) in enumerate(BLOCKS):
                if kind == "F" and ii == 0:
                    if e > 0:
                        s.wait_ge(sbe, 32 * e)
                        s.wait_ge(sa5, sa_base[8 * e])
                    s.dma_start(out=sb_b1e[:, :], in_=b1T[e, :, :]).then_inc(sbe, 16)
                    s.dma_start(out=sb_b2e[:, :], in_=b2T[e, :, :]).then_inc(sbe, 16)
                    s.reg_load(rgf, sb_cct[0:1, e:e + 1])
                    s.reg_load(rgt_, sb_ctt[0:1, e:e + 1])
                rgx = rgf if kind == "F" else rgt_
                guards = s.If_cmp(rgx, ii, "IS_GT")
                guards.__enter__()
                if kind == "F":
                    if b > 0:
                        s.wait_ge(sd5, sd_base[b])
                        s.wait_ge(sbf, 16 * (nF_upto[b] - 1))
                    s.dma_start(out=sb_bfin[:, :].rearrange("p (j w) -> p j w", j=4),
                                in_=bufd[e * REG + ii * 512:e * REG + (ii + 1) * 512, :]
                                .rearrange("(j p) w -> p j w", p=128)).then_inc(sbf, 16)
                if kind == "F":
                    for f in range(8):
                        gidx = 8 * b + f
                        if gidx >= 2:
                            gp_ = gidx - 2
                            bp, fp = gp_ // 8, gp_ % 8
                            s.wait_ge(st5, st_base[bp] + w1_stage(BLOCKS[bp][0], fp))
                            s.wait_ge(sw1a if f % 2 == 0 else sw1b, 16 * (gp_ // 2 + 1))
                        s.dma_start(
                            out=sb_w1b[:, (f % 2) * 4096:(f % 2 + 1) * 4096],
                            in_=w1[e, f, :, :],
                        ).then_inc(sw1a if f % 2 == 0 else sw1b, 16)
                    for f in range(8):
                        gidx = 8 * b + f
                        if gidx >= 2:
                            gp_ = gidx - 2
                            bp, fp = gp_ // 8, gp_ % 8
                            s.wait_ge(st5, st_base[bp] + w2_stage(BLOCKS[bp][0], fp))
                            s.wait_ge(sw2a if f % 2 == 0 else sw2b, 16 * (gp_ // 2 + 1))
                        s.dma_start(
                            out=sb_w2b[:, (f % 2) * 4096:(f % 2 + 1) * 4096],
                            in_=w2[e, f, :, :],
                        ).then_inc(sw2a if f % 2 == 0 else sw2b, 16)
                else:
                    kp = BLOCKS[b - 1][0]
                    T5b = st_base[b]
                    TP5 = st_base[b - 1]
                    if kp == "F":
                        og = [TP5 + w2_stage(kp, 3), TP5 + w2_stage(kp, 5),
                              TP5 + w2_stage(kp, 7)]
                    else:
                        og = [TP5 + w2_stage(kp, 7), TP5 + w2_stage(kp, 3),
                              TP5 + w2_stage(kp, 4)]
                    w1g = [None, None, og[0], og[1],
                           og[2], T5b + 2, T5b + 3, T5b + 4]
                    for f in range(8):
                        if w1g[f] is None:
                            gp_ = 8 * b + f - 2
                            bp, fp = gp_ // 8, gp_ % 8
                            s.wait_ge(st5, st_base[bp] + w1_stage(BLOCKS[bp][0], fp))
                        else:
                            s.wait_ge(st5, w1g[f])
                        s.wait_ge(stl1p[f % 2], 64 * (nT_upto[b] - 1) + 16 * (f // 2))
                        wb_, wo_ = tslot(sb_w1b, f)
                        s.dma_start(out=wb_[:, wo_:wo_ + 4096],
                                    in_=w1[e, f, :, :]).then_inc(stl1p[f % 2], 16)
                    w2g = [None, None, T5b + 9, T5b + 5, T5b + 6,
                           T5b + 10, T5b + 11, T5b + 12]
                    for f in range(8):
                        if w2g[f] is None:
                            gp_ = 8 * b + f - 2
                            bp, fp = gp_ // 8, gp_ % 8
                            s.wait_ge(st5, st_base[bp] + w2_stage(BLOCKS[bp][0], fp))
                        else:
                            s.wait_ge(st5, w2g[f])
                        s.wait_ge(stl2p[f % 2], 64 * (nT_upto[b] - 1) + 16 * (f // 2))
                        wb_, wo_ = tslot(sb_w2b, f)
                        s.dma_start(out=wb_[:, wo_:wo_ + 4096],
                                    in_=w2[e, f, :, :]).then_inc(stl2p[f % 2], 16)
                    for sem_, slot_ in [(sw1a, 2), (sw1b, 3), (sw2a, 4), (sw2b, 5)]:
                        s.wait_ge(sem_, 64 * b)
                        s.dma_start(out=junkd[b, slot_:slot_ + 1, 0:16],
                                    in_=sb_zero[0:1, 0:16]).then_inc(sem_, 64)
                guards.__exit__(None, None, None)
                with s.Else():
                    if kind == "F":
                        s.wait_ge(sbf, 16 * (nF_upto[b] - 1))
                        s.dma_start(out=junkd[b, 1:2, 0:16], in_=sb_zero[0:1, 0:16]
                                    ).then_inc(sbf, 16)
                    else:
                        for qi_, sem_ in enumerate(stl1p + stl2p):
                            s.wait_ge(sem_, 64 * (nT_upto[b] - 1))
                            s.dma_start(out=junkd[b, 6 + qi_:7 + qi_, 0:16],
                                        in_=sb_zero[0:1, 0:16]).then_inc(sem_, 64)
                    s.wait_ge(sw1a, 64 * b)
                    s.dma_start(out=junkd[b, 2:3, 0:16], in_=sb_zero[0:1, 0:16]
                                ).then_inc(sw1a, 64)
                    s.wait_ge(sw1b, 64 * b)
                    s.dma_start(out=junkd[b, 3:4, 0:16], in_=sb_zero[0:1, 0:16]
                                ).then_inc(sw1b, 64)
                    s.wait_ge(sw2a, 64 * b)
                    s.dma_start(out=junkd[b, 4:5, 0:16], in_=sb_zero[0:1, 0:16]
                                ).then_inc(sw2a, 64)
                    s.wait_ge(sw2b, 64 * b)
                    s.dma_start(out=junkd[b, 5:6, 0:16], in_=sb_zero[0:1, 0:16]
                                ).then_inc(sw2b, 64)

        @blk.tensor
        def _(t):
            t.wait_ge(sed, SB4 + 7)
            rgf = t.alloc_register("p5t_nf")
            rgt2 = t.alloc_register("p5t_nt")
            for b, (kind, e, ii) in enumerate(BLOCKS):
                T5, D5, A5 = st_base[b], sd_base[b], sa_base[b]
                nj, tw = NJ[kind], TW[kind]
                if kind == "F" and ii == 0:
                    t.reg_load(rgf, sb_cct[0:1, e:e + 1])
                    t.reg_load(rgt2, sb_ctt[0:1, e:e + 1])
                guard = t.If_cmp(rgf if kind == "F" else rgt2, ii, "IS_GT")
                guard.__enter__()
                t.wait_ge(sbf if kind == "F" else sbf2,
                          16 * (nF_upto[b] if kind == "F" else nT_upto[b]))
                if b > 0:
                    t.wait_ge(sd5, D5)
                for j in range(nj):
                    if j > 0:
                        t.wait_ge(sd5, D5 + j)
                    for it in range(8):
                        ins = t.transpose(ps_d[:, it * 128:(it + 1) * 128],
                                          sb_bfin[:, j * RW + it * 128:j * RW + (it + 1) * 128],
                                          sb_const[:, :])
                    ins.then_inc(st5, 1)
                t.wait_ge(sd5, D5 + nj)
                for fb in range(8):
                    gidx = 8 * b + fb
                    if kind == "T":
                        t.wait_ge(stl1p[fb % 2], 64 * (nT_upto[b] - 1) + 16 * (fb // 2 + 1))
                        w1sb, wbuf = tslot(sb_w1b, fb)
                    else:
                        t.wait_ge(sw1a if fb % 2 == 0 else sw1b, 16 * (gidx // 2 + 1))
                        w1sb, wbuf = sb_w1b, (fb % 2) * 4096
                    if fb >= 2:
                        t.wait_ge(sa5, A5 + fb - 1)
                    if fb % 2 == 0:
                        hps = [(ps_b, 0), (ps_b, 512), (ps_c, 0), (ps_c, 512)]
                    else:
                        hps = [(ps_d, 0), (ps_d, 512), (ps_a, 0), (ps_x, 0)]
                    for it in range(8):
                        for fs in range(4):
                            ten, col = hps[fs]
                            ins = t.matmul(
                                ten[:, col:col + tw],
                                w1sb[:, wbuf + it * 512 + fs * 128:wbuf + it * 512 + (fs + 1) * 128],
                                sb_xtf[:, it * tw:(it + 1) * tw],
                                start=(it == 0), stop=(it == 7))
                    ins.then_inc(st5, 1)
                t.wait_ge(sa5, A5 + 8)
                for fb2 in range(8):
                    gidx = 8 * b + fb2
                    if kind == "T":
                        t.wait_ge(stl2p[fb2 % 2], 64 * (nT_upto[b] - 1) + 16 * (fb2 // 2 + 1))
                        w2sb, wbuf = tslot(sb_w2b, fb2)
                    else:
                        t.wait_ge(sw2a if fb2 % 2 == 0 else sw2b, 16 * (gidx // 2 + 1))
                        w2sb, wbuf = sb_w2b, (fb2 % 2) * 4096
                    for fs2 in range(4):
                        for dt_ in range(8):
                            ten, col = eo_tile(dt_)
                            ins = t.matmul(
                                ten[:, col:col + tw],
                                w2sb[:, wbuf + fs2 * 1024 + dt_ * 128:wbuf + fs2 * 1024 + (dt_ + 1) * 128],
                                sb_hT[:, (fb2 * 4 + fs2) * tw:(fb2 * 4 + fs2 + 1) * tw],
                                start=(fb2 == 0 and fs2 == 0),
                                stop=(fb2 == 7 and fs2 == 3))
                    ins.then_inc(st5, 1)
                for j in range(nj):
                    t.wait_ge(sd5, D5 + nj + 1 + 3 * j)
                    for dt_ in range(8):
                        ins = t.transpose(ps_d[:, dt_ * 128:(dt_ + 1) * 128],
                                          sb_eoT[:, dt_ * tw + j * 128:dt_ * tw + (j + 1) * 128],
                                          sb_const[:, :])
                    ins.then_inc(st5, 1)
                guard.__exit__(None, None, None)
                with t.Else():
                    if b > 0:
                        t.wait_ge(sd5, D5)
                    t.matmul(ps_x[0:1, 300:301], sb_ones_r[0:1, 0:1],
                             sb_ones_r[0:1, 0:1], start=True, stop=True
                             ).then_inc(st5, S_PE[kind])

        @blk.scalar
        def _(sc):
            sc = AutoDrain(sc)
            sc.wait_ge(sed, SB4 + 7)
            rgf = sc.alloc_register("p5a_nf")
            rgt2 = sc.alloc_register("p5a_nt")
            for b, (kind, e, ii) in enumerate(BLOCKS):
                T5, D5, A5 = st_base[b], sd_base[b], sa_base[b]
                nj, tw = NJ[kind], TW[kind]
                if kind == "F" and ii == 0:
                    sc.wait_ge(sbe, 32 * (e + 1))
                    sc.reg_load(rgf, sb_cct[0:1, e:e + 1])
                    sc.reg_load(rgt2, sb_ctt[0:1, e:e + 1])
                guard = sc.If_cmp(rgf if kind == "F" else rgt2, ii, "IS_GT")
                guard.__enter__()
                for fb in range(8):
                    sc.wait_ge(st5, T5 + w1_stage(kind, fb))
                    if fb % 2 == 0:
                        hps = [(ps_b, 0), (ps_b, 512), (ps_c, 0), (ps_c, 512)]
                    else:
                        hps = [(ps_d, 0), (ps_d, 512), (ps_a, 0), (ps_x, 0)]
                    for fs in range(4):
                        ten, col = hps[fs]
                        sc.activation(out=sb_hT[:, (fb * 4 + fs) * tw:(fb * 4 + fs + 1) * tw],
                                      in_=ten[:, col:col + tw], func=AF.Gelu,
                                      bias=sb_b1e[:, fb * 4 + fs:fb * 4 + fs + 1])
                    sc.drain().then_inc(sa5, 1)
                for j in range(nj):
                    sc.wait_ge(sd5, D5 + nj + 1 + 3 * j + 1)
                    sc.activation(out=sb_sq2[:, :], in_=sb_wb[:, :], func=AF.Square,
                                  accum_out=sb_r5[:, 1:2])
                    sc.drain().then_inc(sa5, 1)
                    sc.wait_ge(sd5, D5 + nj + 1 + 3 * j + 2)
                    sc.activation(out=sb_r5[:, 4:5], in_=sb_r5[:, 3:4], func=AF.Sqrt,
                                  bias=sb_eps[:, :])
                    sc.drain().then_inc(sa5, 1)
                guard.__exit__(None, None, None)
                with sc.Else():
                    sc.drain().then_inc(sa5, S_AC[kind])

        @blk.vector
        def _(v):
            v = AutoDrain(v)
            v.wait_ge(sed, SB4 + 7)
            rgf = v.alloc_register("p5v_nf")
            rgt2 = v.alloc_register("p5v_nt")
            for b, (kind, e, ii) in enumerate(BLOCKS):
                T5, D5, A5 = st_base[b], sd_base[b], sa_base[b]
                nj, tw = NJ[kind], TW[kind]
                if kind == "F" and ii == 0:
                    v.reg_load(rgf, sb_cct[0:1, e:e + 1])
                    v.reg_load(rgt2, sb_ctt[0:1, e:e + 1])
                guardv = v.If_cmp(rgf if kind == "F" else rgt2, ii, "IS_GT")
                guardv.__enter__()
                for j in range(nj):
                    v.wait_ge(st5, T5 + j + 1)
                    for it in range(8):
                        v.tensor_copy(out=sb_xtf[:, it * tw + j * 128:it * tw + (j + 1) * 128],
                                      in_=ps_d[:, it * 128:(it + 1) * 128])
                    v.drain().then_inc(sd5, 1)
                v.wait_ge(st5, T5 + w2_stage(kind, 7))
                for dt_ in range(8):
                    ten, col = eo_tile(dt_)
                    v.tensor_scalar(out=sb_eoT[:, dt_ * tw:(dt_ + 1) * tw],
                                    in0=ten[:, col:col + tw],
                                    scalar1=sb_b2e[:, dt_:dt_ + 1], scalar2=None,
                                    op0=ALU.add)
                v.drain().then_inc(sd5, 1)
                for j in range(nj):
                    v.wait_ge(st5, T5 + S_PE[kind] - nj + j + 1)
                    if sc_base[b] + j > 0:
                        v.wait_ge(swb, 16 * (sc_base[b] + j))
                    if kind == "F":
                        v.tensor_scalar_add(sb_r5[:, 6:7], sb_iota_c[:, :],
                                            float(ii * 512 + j * 128))
                    else:
                        v.tensor_tensor(out=sb_r5[:, 6:7], in0=sb_iota_c[:, :],
                                        in1=sb_sbase[:, (e * 4 + ii):(e * 4 + ii) + 1],
                                        op=ALU.add)
                    v.tensor_tensor(out=sb_r5[:, 7:8], in0=sb_r5[:, 6:7],
                                    in1=sb_thrbc[:, e:e + 1], op=ALU.is_lt)
                    v.tensor_tensor(out=sb_r5[:, 8:9], in0=sb_r5[:, 7:8],
                                    in1=sb_bfin[:, j * RW + 1024:j * RW + 1025],
                                    op=ALU.mult)
                    v.scalar_tensor_tensor(out=sb_wb[:, :], in0=ps_d[:, :],
                                           scalar=sb_r5[:, 8:9],
                                           in1=sb_bfin[:, j * RW:j * RW + 1024],
                                           op0=ALU.mult, op1=ALU.add)
                    v.reduce_sum(sb_r5[:, 0:1], sb_wb[:, :], axis=AX.X)
                    v.tensor_scalar_mul(sb_r5[:, 0:1], sb_r5[:, 0:1], 1.0 / 1024)
                    v.drain().then_inc(sd5, 1)
                    v.wait_ge(sa5, A5 + 8 + 2 * j + 1)
                    v.tensor_tensor(out=sb_r5[:, 2:3], in0=sb_r5[:, 0:1],
                                    in1=sb_r5[:, 0:1], op=ALU.mult)
                    v.tensor_scalar_mul(sb_r5[:, 3:4], sb_r5[:, 1:2], 1.0 / 1024)
                    v.tensor_tensor(out=sb_r5[:, 3:4], in0=sb_r5[:, 3:4],
                                    in1=sb_r5[:, 2:3], op=ALU.subtract)
                    v.drain().then_inc(sd5, 1)
                    v.wait_ge(sa5, A5 + 8 + 2 * j + 2)
                    v.reciprocal(sb_r5[:, 5:6], sb_r5[:, 4:5])
                    v.tensor_scalar(out=sb_wb2[:, :], in0=sb_wb[:, :],
                                    scalar1=sb_r5[:, 0:1], scalar2=sb_r5[:, 5:6],
                                    op0=ALU.subtract, op1=ALU.mult)
                    v.tensor_tensor(out=sb_r5[:, 9:10], in0=sb_r5[:, 6:7],
                                    in1=sb_cntbc[:, e:e + 1], op=ALU.is_lt)
                    v.tensor_tensor(out=sb_r5[:, 10:11],
                                    in0=sb_bfin[:, j * RW + 1025:j * RW + 1026],
                                    in1=sb_r5[:, 9:10], op=ALU.mult)
                    v.tensor_scalar(out=sb_r5[:, 11:12], in0=sb_r5[:, 9:10],
                                    scalar1=-float(T), scalar2=float(T),
                                    op0=ALU.mult, op1=ALU.add)
                    v.tensor_tensor(out=sb_r5[:, 10:11], in0=sb_r5[:, 10:11],
                                    in1=sb_r5[:, 11:12], op=ALU.add)
                    v.tensor_copy(out=sb_dsw[:, :], in_=sb_r5[:, 10:11])
                    v.drain().then_inc(sd5, 1)
                guardv.__exit__(None, None, None)
                with v.Else():
                    v.drain().then_inc(sd5, S_DV[kind])

        @blk.gpsimd
        def _(g):
            g.wait_ge(sed, SB4 + 7)
            rgf = g.alloc_register("p5g_nf")
            rgt2 = g.alloc_register("p5g_nt")
            for b, (kind, e, ii) in enumerate(BLOCKS):
                D5 = sd_base[b]
                nj = NJ[kind]
                if kind == "F" and ii == 0:
                    g.reg_load(rgf, sb_cct[0:1, e:e + 1])
                    g.reg_load(rgt2, sb_ctt[0:1, e:e + 1])
                with g.If_cmp(rgf if kind == "F" else rgt2, ii, "IS_GT"):
                    if kind == "T":
                        if b > 0:
                            g.wait_ge(sd5, D5)
                            g.wait_ge(sbf2, 16 * (nT_upto[b] - 1))
                        g.indirect_dma_start(
                            out=sb_bfin[:, 0:RW], out_offset=None,
                            in_=bufd[:, :],
                            in_offset=IndirectOffsetOnAxis(
                                sb_tidx[:, (e * 4 + ii):(e * 4 + ii) + 1], 0),
                        ).then_inc(sbf2, 16)
                    for j in range(nj):
                        g.wait_ge(sd5, D5 + nj + 1 + 3 * j + 3)
                        g.indirect_dma_start(
                            out=yb[:, :],
                            out_offset=IndirectOffsetOnAxis(sb_dsw[:, :], 0),
                            in_=sb_wb2[:, :], in_offset=None,
                        ).then_inc(swb, 16)
                with g.Else():
                    if kind == "T":
                        g.wait_ge(sbf2, 16 * (nT_upto[b] - 1))
                        g.dma_start(out=junkd[b, 1:2, 0:16], in_=sb_zero[0:1, 0:16]
                                    ).then_inc(sbf2, 16)
                    g.wait_ge(swb, 16 * sc_base[b])
                    g.dma_start(out=junkd[b, 0:1, 0:16], in_=sb_zero[0:1, 0:16]
                                ).then_inc(swb, 16 * NSC[kind])
            pass

    with nc.Block() as blkf:
        @blkf.sync
        def _(s):
            s.wait_ge(swb, 16 * sc_base[NBLK])
            s.dma_start(out=y[0:768, :].rearrange("(n p) c -> p n c", p=128),
                        in_=yb[0:768, :].rearrange("(n p) c -> p n c", p=128)
                        ).then_inc(sfin, 16)

        @blkf.scalar
        def _(sc):
            sc.wait_ge(swb, 16 * sc_base[NBLK])
            sc.dma_start(out=y[768:1536, :].rearrange("(n p) c -> p n c", p=128),
                         in_=yb[768:1536, :].rearrange("(n p) c -> p n c", p=128)
                         ).then_inc(sfin, 16)

        @blkf.gpsimd
        def _(g):
            g.wait_ge(swb, 16 * sc_base[NBLK])
            g.dma_start(out=y[1536:2048, :].rearrange("(n p) c -> p n c", p=128),
                        in_=yb[1536:2048, :].rearrange("(n p) c -> p n c", p=128)
                        ).then_inc(sfin2, 16)
            g.wait_ge(sfin, 32)
            g.wait_ge(sfin2, 16)

    return nc


_NC_CACHE = {}


def _get_nc():
    if "nc" not in _NC_CACHE:
        _NC_CACHE["nc"] = build_nc(debug=0)
    return _NC_CACHE["nc"]


def _stage(inp):
    x0 = np.ascontiguousarray(
        np.asarray(inp["L0_attn_hidden_states"], np.float32).reshape(16384, 1024))
    Wsm = np.concatenate(
        [np.asarray(inp["L0_attn_Wb"], np.float32),
         np.asarray(inp["L0_attn_Wfd"], np.float32),
         np.asarray(inp["L0_attn_Wsd"], np.float32),
         np.asarray(inp["L0_attn_Wfg"], np.float32),
         np.asarray(inp["L0_attn_Wsg"], np.float32),
         np.asarray(inp["L0_ffn_router_w"], np.float32).T], axis=1).astype(np.float32)
    wsmb = np.zeros((1, 88), np.float32)
    wsmb[0, 16:32] = np.asarray(inp["L0_attn_fd_bias"], np.float32)
    wsmb[0, 32:48] = np.asarray(inp["L0_attn_sd_bias"], np.float32)
    common = {
        "wq": np.ascontiguousarray(np.asarray(inp["L0_attn_Wq"], np.float32)),
        "wk": np.ascontiguousarray(np.asarray(inp["L0_attn_Wk"], np.float32)),
        "wv": np.ascontiguousarray(np.asarray(inp["L0_attn_Wv"], np.float32)),
        "wsm": np.ascontiguousarray(Wsm),
        "wsmb": wsmb,
        "wo": np.ascontiguousarray(np.asarray(inp["L0_attn_Wo"], np.float32)),
        "woT": np.ascontiguousarray(np.asarray(inp["L0_attn_Wo"], np.float32).T),
        "rT": np.ascontiguousarray(np.asarray(inp["L0_ffn_router_w"], np.float32).T),
        "w1": np.ascontiguousarray(
            np.asarray(inp["L0_ffn_w1"], np.float32)
            .reshape(8, 8, 128, 8, 512).transpose(0, 3, 2, 1, 4)
            .reshape(8, 8, 128, 4096)),
        "w2": np.ascontiguousarray(
            np.asarray(inp["L0_ffn_w2"], np.float32)
            .reshape(8, 8, 4, 128, 1024).transpose(0, 1, 3, 2, 4)
            .reshape(8, 8, 128, 4096)),
        "b1T": np.ascontiguousarray(
            np.asarray(inp["L0_ffn_b1"], np.float32).reshape(8, 32, 128).transpose(0, 2, 1)),
        "b2T": np.ascontiguousarray(
            np.asarray(inp["L0_ffn_b2"], np.float32).reshape(8, 8, 128).transpose(0, 2, 1)),
    }
    in_maps = []
    for c in range(NCORE):
        sl = x0[T * c: T * (c + 1)]
        xTc = np.ascontiguousarray(sl.reshape(NCH, 128, 1024).transpose(0, 2, 1))
        segp = np.zeros((32, 8), np.float32)
        for j in range(NCORE):
            if j < c and j // 4 == c // 4:
                segp[:, j] = 1.0
        cntp = np.zeros((8, 1), np.float32)
        cntp[:c, 0] = 1.0
        m = dict(common)
        m.update({"xT": xTc, "x0r": np.ascontiguousarray(sl),
                  "segp": segp, "cntp": cntp})
        in_maps.append(m)
    return in_maps


def kernel(**inputs):
    from concourse.bass_utils import run_bass_kernel_spmd
    in_maps = _stage(inputs)
    nc = _get_nc()
    res = run_bass_kernel_spmd(nc, in_maps, core_ids=list(range(NCORE)))
    out = np.concatenate([res.results[c]["y"] for c in range(NCORE)], axis=0)
    return out.reshape(2, 8192, 1024)
